# version 07
# baseline (speedup 1.0000x reference)
import math
import numpy as np
import ml_dtypes
import concourse.bass as bass
import concourse.mybir as mybir
from concourse.bass_utils import run_bass_kernel_spmd

F32 = mybir.dt.float32
BF16 = mybir.dt.bfloat16
AF = mybir.ActivationFunctionType
ALU = mybir.AluOpType
AX = mybir.AxisListType

BF = ml_dtypes.bfloat16

D = 2048
T = 1024
NTILE = D // 128
S = 2048
LAMBDA_INIT = 0.8 - 0.6 * math.exp(-0.3 * 2)

ENGINES = ("pe", "act", "dve", "pool", "sync")


class Builder:

    def __init__(self):
        self.q = {e: [] for e in ENGINES}
        self.cnt = {}
        self.order = []

    def sem(self, name):
        if name not in self.cnt:
            self.cnt[name] = 0
            self.order.append(name)
        return name

    def v(self, name):
        self.sem(name)
        return self.cnt[name]

    def op(self, eng, fn, waits=(), inc=None, n=1):
        w = tuple((s, v) for (s, v) in waits if v > 0)
        for s, _ in w:
            self.sem(s)
        if inc is not None:
            self.sem(inc)
            self.cnt[inc] += n
            post = (inc, n)
        else:
            post = None
        self.q[eng].append((fn, w, post))
        return self.cnt[inc] if inc is not None else None

    def emit(self, nc, sems):
        with nc.Block() as block:
            def mk(name):
                def run(e):
                    for fn, waits, post in self.q[name]:
                        for s, v in waits:
                            e.wait_ge(sems[s], v)
                        ins = fn(e)
                        if post is not None:
                            assert ins is not None, "inc-op returned None"
                            ins.then_inc(sems[post[0]], post[1])
                return run
            block.tensor(mk("pe"))
            block.scalar(mk("act"))
            block.vector(mk("dve"))
            block.gpsimd(mk("pool"))
            block.sync(mk("sync"))


def own_pos(h):
    if h == 0:
        return np.concatenate([np.arange(1536, 2048), np.arange(0, 512)])
    return np.concatenate([np.arange(1024, 1536), np.arange(512, 1024)])


def rope0_tables(h):
    pos = own_pos(h).astype(np.float64)
    invf = 10000.0 ** (-np.arange(64) / 64.0)
    fr = pos[None, :] * invf[:, None]
    c = np.cos(fr); s = np.sin(fr)
    C = np.concatenate([c, c], 0).astype(BF)
    blk = np.concatenate([s[32:64], -s[0:32]], 0)
    S2 = np.concatenate([blk, blk], 0).astype(BF)
    return C, S2


def rope1_tables(h):
    pos = own_pos(h).astype(np.float64)
    invf = 1.0 / 10000.0 ** (np.arange(0, 64, 2) / 64.0)
    fr = pos[None, :] * invf[:, None]
    c = np.cos(fr); s = np.sin(fr)
    C1 = np.concatenate([c, c], 0).astype(BF)
    SS = np.concatenate([-s, s], 0).astype(BF)
    return C1, SS


def mask_tables(h):
    gs = {0: {0: 1536, 1: 0}, 1: {0: 1024, 1: 512}}[h]
    out = np.zeros((16, 128, 512), dtype=BF)
    i = 0
    for slot, kts in ((0, range(8, 16)), (1, range(0, 8))):
        g = gs[slot]
        for kt in kts:
            k = kt * 128 + np.arange(128)[:, None]
            q = g + np.arange(512)[None, :]
            out[i] = (k <= q).astype(np.float32).astype(BF)
            i += 1
    return out


def qpos(j):
    return 2 * (j // 4) + (j % 2), 64 * ((j // 2) % 2)


def q_perm_rows():
    order = [None] * 32
    for j in range(32):
        t, base = qpos(j)
        order[2 * t + base // 64] = j
    return order


MASKED = {}
for _i, (_slot, _kt) in enumerate([(0, k) for k in range(8, 16)] +
                                  [(1, k) for k in range(0, 8)]):
    MASKED[(_slot, _kt)] = _i
SLOT_KTS = {0: list(range(16)), 1: list(range(8))}


def chunked_cols(dram_ap, col0, ncols):
    return dram_ap[:, col0:col0 + ncols].rearrange("(c p) n -> p c n", p=128)


def sb3(t, nch, ncols):
    return t.ap().rearrange("p (c n) -> p c n", c=nch)


class _T:
    def __init__(self, ap):
        self._ap = ap

    def ap(self):
        return self._ap


class Prog:
    def __init__(self, name, phased=False):
        self.nc = bass.Bass("TRN2", target_bir_lowering=False, dynamic_dma_scratch_size=16384,
                            num_devices=8 if phased else None)
        self.b = Builder()
        self.name = name
        self.banks = [self.nc.alloc_psum_tensor("bank%d" % i, [128, 512], F32)
                      for i in range(8)]
        self.out_waits = []
        self.phased = phased
        self.phase = ""
        self._pbase = 16512
        self._poff = 0
        self._CAP = 229344

    def sb(self, name, shape, dtype):
        if not self.phased:
            return self.nc.alloc_sbuf_tensor(name, shape, dtype)
        if getattr(self, "_G", None) is None:
            gcols = (self._CAP - 16512) // 2
            self._G = self.nc.alloc_sbuf_tensor_at("Gback", [128, gcols],
                                                   BF16, offset=16512)
            self._Gb = self._G.ap()
            self._Gf = self._Gb.bitcast(F32)
            self._Gi = self._Gb.bitcast(mybir.dt.int32)
        esz = 2 if dtype == BF16 else 4
        nbytes = int(np.prod(shape[1:])) * esz
        nbytes = (nbytes + 31) & ~31
        off = self._pbase + self._poff
        assert off + nbytes <= self._CAP, \
            "phase %s overflow: %d" % (self.phase, off + nbytes)
        self._poff += nbytes
        rel = off - 16512
        assert rel % esz == 0
        if dtype == BF16:
            base = self._Gb
        elif dtype == F32:
            base = self._Gf
        else:
            base = self._Gi
        return _T(base[0:shape[0], rel // esz: rel // esz + int(shape[1])])

    def new_phase(self, tag):
        assert self.phased
        b = self.b
        allw = [(s, b.v(s)) for s in b.order if b.v(s) > 0]
        ls = self.ln_small.ap()
        fill = {
            "pe": lambda e: e.matmul(self.banks[0].ap()[0:1, 0:1],
                                     self.c_ones_col.ap()[0:1, 0:1],
                                     self.c_ones_col.ap()[0:1, 0:1],
                                     start=True, stop=True),
            "act": lambda e: e.copy(out=ls[0:1, 0:4],
                                    in_=self.c_ones_row.ap()[0:1, 0:4]),
            "dve": lambda e: e.memset(ls[0:1, 4:8], 0.0),
            "pool": lambda e: e.dma_start(out=ls[0:1, 8:12],
                                          in_=ls[0:1, 12:16]),
            "sync": lambda e: e.dma_start(out=ls[0:1, 16:20],
                                          in_=ls[0:1, 20:24]),
        }
        for eng in ENGINES:
            for i in range(0, len(allw), 6):
                b.op(eng, lambda e: None, waits=tuple(allw[i:i + 6]))
                b.op(eng, fill[eng], inc="phf",
                     n=16 if eng in ("pool", "sync") else 1)
        self.phase = tag
        self._poff = 0

    def din(self, name, shape, dtype):
        return self.nc.dram_tensor(name, shape, dtype, kind="ExternalInput").ap()

    def dout(self, name, shape, dtype):
        return self.nc.dram_tensor(name, shape, dtype, kind="ExternalOutput").ap()

    def make_common(self, rope=True):
        nc = self.nc
        self.c_ones_col = self.sb("ones_col", [128, 1], BF16)
        self.c_ones_row = self.sb("ones_row", [1, 128], BF16)
        if rope:
            self.qtmp = self.sb("qtmp", [128, 2 * T], BF16)
            self.ropetmp = self.sb("ropetmp", [128, T], BF16)
        self.ln_small = self.sb("ln_small", [1, 2048], BF16)
        if self.phased:
            self._pbase = self._pbase + self._poff
            self._poff = 0
        b = self.b
        b.op("dve", lambda e: e.memset(self.c_ones_col.ap()[:, :], 1.0))
        b.op("dve", lambda e: e.memset(self.c_ones_row.ap()[:, :], 1.0),
             inc="dve")
        self.consts_dve = b.v("dve")

    def finish(self):
        b, nc = self.b, self.nc
        import contextlib
        stack = contextlib.ExitStack()
        sems = {}
        for name in b.order:
            sems[name] = stack.enter_context(nc.semaphore(name))
        b.emit(nc, sems)
        stack.close()
        return nc



def emit_layernorm(p, xt3, xn3, st, rbuf, entry=(), chunk_scale=None):
    b = p.b
    A = p.banks
    for c in range(NTILE):
        b.op("act", lambda e, c=c: e.copy(out=xn3[:, c, :], in_=xt3[:, c, :]),
             waits=(entry if c == 0 else ()),
             inc="act" if c == NTILE - 1 else None)
    a1 = b.v("act")
    for half in range(2):
        for c in range(NTILE):
            b.op("pe", lambda e, c=c, half=half: e.matmul(
                A[half].ap()[0:1, :], p.c_ones_col.ap()[:, 0:1],
                xn3[:, c, half * 512:(half + 1) * 512],
                start=(c == 0), stop=(c == NTILE - 1)),
                waits=((("act", a1), ("dve", getattr(p, "consts_dve", 0)))
                       if (half == 0 and c == 0) else ()),
                inc="pe" if (half == 1 and c == NTILE - 1) else None)
    p1 = b.v("pe")
    for c in range(NTILE):
        b.op("act", lambda e, c=c: e.activation(
            out=xn3[:, c, :], in_=xt3[:, c, :], func=AF.Square),
            waits=((("pe", p1),) if c == 0 else ()),
            inc="act" if c == NTILE - 1 else None)
    a2 = b.v("act")
    if getattr(p, "dbg_sq", None) is not None:
        _dsq, _dxt = p.dbg_sq, p.dbg_xt
        b.op("sync", lambda e, _dsq=_dsq: e.dma_start(
            out=_dsq.rearrange("(c p) t -> p c t", p=128), in_=xn3),
            waits=(("act", a2),), inc="dbgq", n=16)
        b.op("sync", lambda e, _dxt=_dxt: e.dma_start(
            out=_dxt.rearrange("(c p) t -> p c t", p=128), in_=xt3),
            inc="dbgq", n=16)
        dv = b.v("dbgq")
        b.op("pe", lambda e: None, waits=(("dbgq", dv),))
        p.dbg_sq = None
    for half in range(2):
        for c in range(NTILE):
            b.op("pe", lambda e, c=c, half=half: e.matmul(
                A[6 + half].ap()[0:1, :], p.c_ones_col.ap()[:, 0:1],
                xn3[:, c, half * 512:(half + 1) * 512],
                start=(c == 0), stop=(c == NTILE - 1)),
                waits=((("act", a2),) if (half == 0 and c == 0) else ()),
                inc="pe" if (half == 1 and c == NTILE - 1) else None)
    p2 = b.v("pe")
    sm = st[0:1, 0:1024]; ex = st[0:1, 1024:2048]; var = st[0:1, 2048:3072]
    for half in range(2):
        b.op("dve", lambda e, half=half: e.tensor_scalar_mul(
            sm[:, half * 512:(half + 1) * 512], A[half].ap()[0:1, :], 1.0 / 2048.0),
            waits=((("pe", p2),) if half == 0 else ()))
        b.op("dve", lambda e, half=half: e.tensor_scalar_mul(
            ex[:, half * 512:(half + 1) * 512], A[6 + half].ap()[0:1, :],
            1.0 / 2048.0), inc="dve" if half == 1 else None)
    d1 = b.v("dve")
    b.op("dve", lambda e: e.tensor_mul(var, sm, sm), waits=(("dve", d1),),
         inc="dve")
    d1 = b.v("dve")
    b.op("dve", lambda e: e.tensor_scalar(var, var, -1.0, 1e-5, ALU.mult,
                                          ALU.add), waits=(("dve", d1),),
         inc="dve")
    d1 = b.v("dve")
    b.op("dve", lambda e: e.tensor_add(var, var, ex), waits=(("dve", d1),),
         inc="dve")
    d1 = b.v("dve")
    b.op("act", lambda e: e.activation(out=var, in_=var, func=AF.Sqrt),
         waits=(("dve", d1),), inc="act")
    a3 = b.v("act")
    b.op("dve", lambda e: e.reciprocal(var, var), waits=(("act", a3),),
         inc="dve")
    d1 = b.v("dve")
    b.op("dve", lambda e: e.tensor_mul(sm, sm, var), waits=(("dve", d1),),
         inc="dve")
    d1 = b.v("dve")
    b.op("dve", lambda e: e.tensor_copy(p.ln_small.ap()[0:1, 0:1024], var),
         waits=(("dve", d1),))
    b.op("dve", lambda e: e.tensor_copy(p.ln_small.ap()[0:1, 1024:2048], sm),
         inc="dve")
    d2 = b.v("dve")
    for j in range(4):
        b.op("pe", lambda e, j=j: e.matmul(
            A[4 + j].ap()[:, :], p.c_ones_row.ap()[0:1, :],
            p.ln_small.ap()[0:1, j * 512:(j + 1) * 512], start=True, stop=True),
            waits=((("dve", d2),) if j == 0 else ()),
            inc="pe" if j == 3 else None)
    p3 = b.v("pe")
    for j in range(4):
        b.op("act", lambda e, j=j: e.copy(
            out=rbuf[:, j * 512:(j + 1) * 512], in_=A[4 + j].ap()[:, :]),
            waits=((("pe", p3),) if j == 0 else ()),
            inc="act" if j == 3 else None)
    a4 = b.v("act")
    for c in range(NTILE):
        b.op("dve", lambda e, c=c: e.tensor_mul(
            xt3[:, c, :], xt3[:, c, :], rbuf[:, 0:1024]),
            waits=(("act", a4),) if c == 0 else (), inc="dve")
        dmw = b.v("dve")
        b.op("dve", lambda e, c=c: e.tensor_sub(
            xn3[:, c, :], xt3[:, c, :], rbuf[:, 1024:2048]),
            waits=(("dve", dmw),), inc="dve")
    if chunk_scale is not None:
        dmw = b.v("dve")
        for c in range(NTILE):
            b.op("dve", lambda e, c=c: e.tensor_scalar(
                xn3[:, c, :], xn3[:, c, :], chunk_scale[:, c:c + 1], None,
                ALU.mult), waits=((("dve", dmw),) if c == 0 else ()),
                inc="dve" if c == NTILE - 1 else None)
    return b.v("dve")


def emit_rope(p, kind, src, dst, tabs=None):
    b = p.b
    tmp = p.ropetmp.ap()
    if kind == "l0":
        if tabs is not None:
            C, S2 = tabs
        else:
            C = p.c_rope0c.ap(); S2 = p.c_rope0s.ap()
        b.op("dve", lambda e: e.tensor_mul(dst, src, C))
        for g in (0, 64):
            b.op("dve", lambda e, g=g: e.tensor_mul(
                tmp[g:g + 32, :], src[g + 32:g + 64, :], S2[g + 32:g + 64, :]))
            b.op("dve", lambda e, g=g: e.tensor_mul(
                tmp[g + 32:g + 64, :], src[g:g + 32, :], S2[g:g + 32, :]),
                inc="dve" if g == 64 else None)
        dmw = b.v("dve")
        b.op("dve", lambda e: e.tensor_add(dst, dst, tmp[:, :]),
             waits=(("dve", dmw),), inc="dve")
    else:
        C = p.c_rope1c.ap(); SS = p.c_rope1s.ap()
        b.op("dve", lambda e: e.tensor_copy(dst[64:128, :], src[64:128, :]))
        b.op("dve", lambda e: e.tensor_mul(dst[0:32, :], src[0:32, :], C[0:32, :]))
        b.op("dve", lambda e: e.tensor_mul(
            tmp[0:32, :], src[32:64, :], SS[32:64, :]))
        b.op("dve", lambda e: e.tensor_add(dst[0:32, :], dst[0:32, :], tmp[0:32, :]))
        b.op("dve", lambda e: e.tensor_mul(
            dst[32:64, :], src[32:64, :], C[32:64, :]))
        b.op("dve", lambda e: e.tensor_mul(
            tmp[32:64, :], src[0:32, :], SS[0:32, :]))
        b.op("dve", lambda e: e.tensor_add(
            dst[32:64, :], dst[32:64, :], tmp[32:64, :]), inc="dve")
    return b.v("dve")


def emit_proj_fm(p, w_dram, xn, out_slab, dout, wbuf, xn_ready, rope=None,
                 sem_prefix="q", post_tile=None, spans=None, out_w=T,
                 rope_tabs=None, ring_out=None):
    b = p.b
    ndt = dout // 128
    def _v3(t, c, w):
        return sb3(t, c, w) if callable(getattr(t, "ap", None)) \
            else t.rearrange("p (c t) -> p c t", c=c)
    if spans is None:
        spans = [(_v3(xn, NTILE, T), 0)]
    nspan = len(spans)
    assert nspan <= 2
    nslots = ring_out[0] if ring_out else ndt
    out3 = _v3(out_slab, nslots, out_w)
    wb3 = wbuf.ap()[:, 0:2 * NTILE * 128].rearrange("p (c n) -> p c n",
                                                    c=2 * NTILE)
    qt3 = sb3(p.qtmp, 2, T)
    A = p.banks
    evac = {}
    rope_m = {}
    entry_act = b.v("act")
    entry_pe = b.v("pe")
    entry_dve = b.v("dve")
    nb = 2 * nspan
    for dt in range(ndt):
        buf = dt % 2
        wsem = "%s_w%d" % (sem_prefix, buf)
        waits = []
        if dt < 2:
            waits.append(("pe", entry_pe))
        if dt >= 2:
            waits.append(("pe", evac[dt - 2][0]))
        b.op("pool", lambda e, dt=dt, buf=buf: e.dma_start(
            out=wb3[:, buf * NTILE:(buf + 1) * NTILE, :],
            in_=chunked_cols(w_dram, dt * 128, 128)),
            waits=waits, inc=wsem, n=16)
        wv = b.v(wsem)
        pe_w = [(wsem, wv)]
        if dt == 0 and xn_ready:
            pe_w.append(xn_ready)
        if dt < 2:
            pe_w.append(("act", entry_act))
        if dt >= 2:
            pe_w.append(("act", evac[dt - 2][1]))
        for sp in range(nspan):
            xn3 = spans[sp][0]
            for tch in range(2):
                bank = A[(nb * buf + 2 * sp + tch) % 8]
                for kc in range(NTILE):
                    b.op("pe", lambda e, xn3=xn3, buf=buf, tch=tch, kc=kc,
                         bank=bank: e.matmul(
                        bank.ap()[:, :], wb3[:, buf * NTILE + kc, :],
                        xn3[:, kc, tch * 512:(tch + 1) * 512],
                        start=(kc == 0), stop=(kc == NTILE - 1)),
                        waits=pe_w if (sp == 0 and tch == 0 and kc == 0) else (),
                        inc="pe" if (sp == nspan - 1 and tch == 1 and
                                     kc == NTILE - 1) else None)
        pm = b.v("pe")
        aml = []
        for sp in range(nspan):
            cb = spans[sp][1]
            for tch in range(2):
                bank = A[(nb * buf + 2 * sp + tch) % 8]
                if rope:
                    o = qt3[:, buf, tch * 512:(tch + 1) * 512]
                else:
                    o = out3[:, dt % nslots, cb + tch * 512:cb + (tch + 1) * 512]
                w = [("pe", pm)] if (sp == 0 and tch == 0) else []
                if rope and dt < 2 and sp == 0 and tch == 0:
                    w.append(("dve", entry_dve))
                if rope and dt >= 2 and sp == 0 and tch == 0:
                    w.append(("dve", rope_m[dt - 2]))
                if rope and sp >= 1 and tch == 0:
                    w.append(("dve", b.v("dve")))
                b.op("act", lambda e, bank=bank, o=o: e.copy(
                    out=o, in_=bank.ap()[:, :]),
                    waits=w, inc="act" if tch == 1 else None)
            if rope:
                am = b.v("act")
                ring_w = [("act", am)]
                if ring_out and sp == 0 and dt >= nslots:
                    ring_w.append(("qo", ring_out[1] * (dt - nslots + 1)))
                b.op("dve", lambda e: None, waits=ring_w)
                tabs = rope_tabs[sp] if rope_tabs else None
                rope_m[dt] = emit_rope(p, rope, qt3[:, buf, :],
                                       out3[:, dt % nslots, cb:cb + T],
                                       tabs=tabs)
        evac[dt] = (pm, b.v("act"))
        if ring_out is not None:
            ring_out[2](dt, out3[:, dt % nslots, :], ("dve", rope_m[dt]))
            assert b.v("qo") == ring_out[1] * (dt + 1), \
                (b.v("qo"), ring_out[1], dt)
        if post_tile is not None:
            post_tile(dt, evac[dt][1])
    if ring_out is not None:
        return ("qo", b.v("qo"))
    return ("dve", b.v("dve")) if rope else ("act", b.v("act"))


def emit_vproj_tm(p, w_dram, xn, out_slab, dout, wbuf, xn_ready,
                  sem_prefix="v"):
    b = p.b
    xn3 = sb3(xn, NTILE, T)
    nvch = dout // 512
    out3 = sb3(out_slab, 8, dout)
    wb3 = sb3(wbuf, NTILE, 512)
    A = p.banks
    wsem = sem_prefix + "_wdma"
    evac = {}
    entry_act = b.v("act")
    entry_pe = b.v("pe")
    for nch in range(nvch):
        dma_w = []
        if nch == 0:
            dma_w.append(("pe", entry_pe))
            if xn_ready:
                dma_w.append(xn_ready)
        if nch > 0:
            dma_w.append(("pe", evac_pe_last))
        b.op("pool", lambda e, nch=nch: e.dma_start(
            out=wb3[:, 0:NTILE, :],
            in_=chunked_cols(w_dram, nch * 512, 512)),
            waits=dma_w, inc=wsem, n=16)
        wv = b.v(wsem)
        for tt in range(8):
            bank = A[4 + tt % 2]
            pe_w = []
            if tt == 0:
                pe_w.append((wsem, wv))
                if nch == 0:
                    pe_w.append(("act", entry_act))
                    if xn_ready:
                        pe_w.append(xn_ready)
            key = (nch * 8 + tt) - 2
            if key >= 0:
                pe_w.append(("act", evac[key]))
            for kc in range(NTILE):
                b.op("pe", lambda e, tt=tt, kc=kc, bank=bank: e.matmul(
                    bank.ap()[:, :], xn3[:, kc, tt * 128:(tt + 1) * 128],
                    wb3[:, kc, :],
                    start=(kc == 0), stop=(kc == NTILE - 1)),
                    waits=pe_w if kc == 0 else (),
                    inc="pe" if kc == NTILE - 1 else None)
            pm = b.v("pe")
            b.op("act", lambda e, tt=tt, nch=nch, bank=bank: e.copy(
                out=out3[:, tt, nch * 512:(nch + 1) * 512], in_=bank.ap()[:, :]),
                waits=(("pe", pm),), inc="act")
            evac[nch * 8 + tt] = b.v("act")
        evac_pe_last = b.v("pe")
    return ("act", b.v("act"))



def build_l0_qkv(debug=False, p=None, io=None, permute_q=False):
    standalone = p is None
    if standalone:
        p = Prog("l0qkv")
    io = io or {}
    def din(name, shape, dtype):
        return io[name] if name in io else p.din(name, shape, dtype)
    def dout(name, shape, dtype):
        return io[name] if name in io else p.dout(name, shape, dtype)
    b = p.b
    x_in = din("x", [D, T], F32)
    wqT = din("wqT", [D, D], F32)
    wkT = din("wkT", [D, 1024], F32)
    wvT = din("wvT", [D, 1024], F32)
    rc = din("rope0c", [128, T], BF16)
    rs = din("rope0s", [128, T], BF16)
    q_out = dout("q", [D, T], BF16)
    kv_out = dout("kv", [2048, T], BF16)
    xn_out = dout("xn_dbg", [D, T], BF16) if debug else None

    xt = p.sb("xt", [128, NTILE * T], F32)
    xn = p.sb("xn", [128, NTILE * T], BF16)
    qs = p.sb("qs", [128, NTILE * T], BF16)
    ks = p.sb("ks", [128, 8 * T], BF16)
    vs = p.sb("vs", [128, 8 * T], BF16)
    wbuf = p.sb("wbuf", [128, 2 * NTILE * 128], BF16)
    wvbuf = p.sb("wvbuf", [128, NTILE * 512], BF16)
    stats = wvbuf.ap().bitcast(F32)[0:1, 0:3072]
    rbuf = p.sb("rbuf", [128, 2 * T], BF16)
    if standalone:
        p.make_common()
    p.c_rope0c = p.sb("rope0c_t", [128, T], BF16)
    p.c_rope0s = p.sb("rope0s_t", [128, T], BF16)

    b.op("sync", lambda e: e.dma_start(
        out=sb3(xt, NTILE, T), in_=x_in.rearrange("(c p) t -> p c t", p=128)),
        inc="adma", n=16)
    b.op("sync", lambda e: e.dma_start(out=p.c_rope0c.ap()[:, :], in_=rc),
         inc="adma", n=16)
    b.op("sync", lambda e: e.dma_start(out=p.c_rope0s.ap()[:, :], in_=rs),
         inc="adma", n=16)
    ad = b.v("adma")
    xn_ready = emit_layernorm(p, sb3(xt, NTILE, T), sb3(xn, NTILE, T),
                              stats, rbuf.ap(), entry=(("adma", ad),))

    qm = emit_proj_fm(p, wqT, xn, qs, D, wbuf, ("dve", xn_ready), rope="l0",
                      sem_prefix="q")
    km = emit_proj_fm(p, wkT, xn, ks, 1024, wbuf, None, rope="l0",
                      sem_prefix="k")
    vm = emit_vproj_tm(p, wvT, xn, vs, 1024, wvbuf, None, sem_prefix="v")

    if permute_q:
        perm = q_perm_rows()
        for srow in range(32):
            b.op("sync", lambda e, srow=srow, j=perm[srow]: e.dma_start(
                out=q_out[srow * 64:(srow + 1) * 64, :],
                in_=qs.ap().rearrange("p (c t) -> p c t", c=NTILE)[
                    (j % 2) * 64:(j % 2) * 64 + 64, j // 2, :]),
                waits=(qm,) if srow == 0 else (), inc="osem", n=16)
    else:
        b.op("sync", lambda e: e.dma_start(
            out=q_out.rearrange("(c p) t -> p c t", p=128),
            in_=sb3(qs, NTILE, T)),
            waits=(qm,), inc="osem", n=16)
    b.op("sync", lambda e: e.dma_start(
        out=kv_out[0:1024, :].rearrange("(c p) t -> p c t", p=128),
        in_=sb3(ks, 8, T)), waits=(km,), inc="osem", n=16)
    b.op("sync", lambda e: e.dma_start(
        out=kv_out[1024:2048, :].rearrange("(c p) t -> p c t", p=128),
        in_=sb3(vs, 8, T)), waits=(vm,), inc="osem", n=16)
    if debug:
        b.op("sync", lambda e: e.dma_start(
            out=xn_out.rearrange("(c p) t -> p c t", p=128), in_=sb3(xn, NTILE, T)),
            waits=(("dve", xn_ready),), inc="osem", n=16)
    b.op("sync", lambda e: None, waits=(("osem", b.v("osem")),))
    return p



def shard_x(x_full):
    out = []
    for c in range(8):
        bb, h = divmod(c, 2)
        out.append(np.ascontiguousarray(x_full[bb][own_pos(h)].T.astype(np.float32)))
    return out


def kernel(**inputs):
    raise NotImplementedError("assembled incrementally; see scratch drivers")



SQ8192 = 1.0 / math.sqrt(8192.0)


def emit_l0_bounds(p, qs, ks, scr, smax, negBs, ready):
    b = p.b
    A = p.banks
    q3 = qs.rearrange("p (c t) -> p c t", c=NTILE)
    k3 = ks.rearrange("p (c t) -> p c t", c=8)
    sm = smax.ap()
    prev_pe = 0
    for hh in range(NTILE):
        w = list(ready) if hh == 0 else [("pe", prev_pe)]
        b.op("act", lambda e, hh=hh: e.activation(
            out=scr[:, 0:1024], in_=q3[:, hh, :], func=AF.Square),
            waits=w, inc="act")
        am = b.v("act")
        dwait = b.v("dve")
        for g in range(2):
            for tch in range(2):
                b.op("pe", lambda e, g=g, tch=tch: e.matmul(
                    A[4 + 2 * g + tch].ap()[0:1, :],
                    p.c_ones_col.ap()[g * 64:(g + 1) * 64, 0:1],
                    scr[g * 64:(g + 1) * 64, tch * 512:(tch + 1) * 512],
                    start=True, stop=True),
                    waits=((("act", am), ("dve", dwait))
                           if (g == 0 and tch == 0) else ()),
                    inc="pe" if (g == 1 and tch == 1) else None)
        prev_pe = b.v("pe")
        for g in range(2):
            j = q_perm_rows()[2 * hh + g]
            b.op("dve", lambda e, g=g, j=j: e.reduce_max(
                sm[0:1, j:j + 1], A[4 + 2 * g].ap()[0:1, :], axis=AX.X),
                waits=((("pe", prev_pe),) if g == 0 else ()))
            b.op("dve", lambda e, g=g, j=j: e.reduce_max(
                sm[0:1, 64 + j:65 + j], A[5 + 2 * g].ap()[0:1, :], axis=AX.X),
                inc="dve" if g == 1 else None)
    for kk in range(8):
        b.op("act", lambda e, kk=kk: e.activation(
            out=scr[:, :], in_=k3[:, kk, :], func=AF.Square),
            waits=(("pe", prev_pe),), inc="act")
        am = b.v("act")
        for g in range(2):
            dwait = b.v("dve")
            for tch in range(4):
                b.op("pe", lambda e, g=g, tch=tch: e.matmul(
                    A[4 + tch].ap()[0:1, :],
                    p.c_ones_col.ap()[g * 64:(g + 1) * 64, 0:1],
                    scr[g * 64:(g + 1) * 64, tch * 512:(tch + 1) * 512],
                    start=True, stop=True),
                    waits=((("act", am), ("dve", dwait)) if tch == 0 else ()),
                    inc="pe" if tch == 3 else None)
            prev_pe = b.v("pe")
            j = 2 * kk + g
            for tch in range(4):
                slot = (32 + j) if tch == 0 else (96 + j + 32 * (tch - 1))
                b.op("dve", lambda e, tch=tch, slot=slot: e.reduce_max(
                    sm[0:1, slot:slot + 1], A[4 + tch].ap()[0:1, :], axis=AX.X),
                    waits=((("pe", prev_pe),) if tch == 0 else ()),
                    inc="dve" if tch == 3 else None)
    dm = b.v("dve")
    b.op("dve", lambda e: e.tensor_max(sm[0:1, 0:48], sm[0:1, 0:48],
                                       sm[0:1, 64:112]),
         waits=(("dve", dm),), inc="dve")
    dm = b.v("dve")
    b.op("dve", lambda e: e.tensor_max(sm[0:1, 32:48], sm[0:1, 32:48],
                                       sm[0:1, 128:144]),
         waits=(("dve", dm),), inc="dve")
    dm = b.v("dve")
    b.op("dve", lambda e: e.tensor_max(sm[0:1, 32:48], sm[0:1, 32:48],
                                       sm[0:1, 160:176]),
         waits=(("dve", dm),), inc="dve")
    dm = b.v("dve")
    b.op("act", lambda e: e.activation(out=sm[0:1, 64:112], in_=sm[0:1, 0:48],
                                       func=AF.Sqrt),
         waits=(("dve", dm),), inc="act")
    am = b.v("act")
    b.op("dve", lambda e: e.tensor_copy(
        sm[0:1, 128:160].rearrange("p (a b) -> p a b", b=2)[:, :, 0:1],
        sm[0:1, 96:112].rearrange("p (a b) -> p a b", b=1)),
        waits=(("act", am),))
    b.op("dve", lambda e: e.tensor_copy(
        sm[0:1, 128:160].rearrange("p (a b) -> p a b", b=2)[:, :, 1:2],
        sm[0:1, 96:112].rearrange("p (a b) -> p a b", b=1)), inc="dve")
    dm = b.v("dve")
    b.op("dve", lambda e: e.scalar_tensor_tensor(
        sm[0:1, 160:192], sm[0:1, 64:96], -0.125, sm[0:1, 128:160],
        ALU.mult, ALU.mult), waits=(("dve", dm),), inc="dve")
    dm = b.v("dve")
    b.op("dve", lambda e: e.tensor_scalar_add(sm[0:1, 160:192],
                                              sm[0:1, 160:192], 44.0),
         waits=(("dve", dm),), inc="dve")
    dm = b.v("dve")
    b.op("dve", lambda e: e.tensor_copy(p.ln_small.ap()[0:1, 0:32],
                                        sm[0:1, 160:192]),
         waits=(("dve", dm),), inc="dve")
    dm = b.v("dve")
    b.op("pe", lambda e: e.matmul(A[4].ap()[:, 0:32], p.c_ones_row.ap()[0:1, :],
                                  p.ln_small.ap()[0:1, 0:32],
                                  start=True, stop=True),
         waits=(("dve", dm),), inc="pe")
    pm = b.v("pe")
    b.op("act", lambda e: e.copy(out=negBs.ap()[:, 0:32], in_=A[4].ap()[:, 0:32]),
         waits=(("pe", pm),), inc="act")
    return b.v("act")


def emit_l0_lambda(p, lam, sublnw, lsc, lnb, sw):
    b = p.b
    A = p.banks
    sc = lsc
    b.op("sync", lambda e: e.dma_start(out=sc[0:1, 0:256], in_=lam),
         inc="adma", n=16)
    b.op("sync", lambda e: e.dma_start(out=sw[:, 0:1], in_=sublnw),
         inc="adma", n=16)
    ad = b.v("adma")
    b.op("dve", lambda e: e.tensor_mul(sc[0:1, 0:64], sc[0:1, 0:64],
                                       sc[0:1, 64:128]),
         waits=(("adma", ad),))
    b.op("dve", lambda e: e.tensor_mul(sc[0:1, 128:192], sc[0:1, 128:192],
                                       sc[0:1, 192:256]), inc="dve")
    d = b.v("dve")
    b.op("dve", lambda e: e.reduce_sum(sc[0:1, 64:65], sc[0:1, 0:64], axis=AX.X),
         waits=(("dve", d),))
    b.op("dve", lambda e: e.reduce_sum(sc[0:1, 65:66], sc[0:1, 128:192],
                                       axis=AX.X), inc="dve")
    d = b.v("dve")
    b.op("act", lambda e: e.activation(out=sc[0:1, 66:68], in_=sc[0:1, 64:66],
                                       func=AF.Exp),
         waits=(("dve", d),), inc="act")
    a = b.v("act")
    b.op("dve", lambda e: e.tensor_sub(sc[0:1, 68:69], sc[0:1, 67:68],
                                       sc[0:1, 66:67]),
         waits=(("act", a),), inc="dve")
    d = b.v("dve")
    b.op("dve", lambda e: e.tensor_scalar_add(sc[0:1, 68:69], sc[0:1, 68:69],
                                              -LAMBDA_INIT),
         waits=(("dve", d),), inc="dve")
    d = b.v("dve")
    b.op("dve", lambda e: e.tensor_copy(p.ln_small.ap()[0:1, 40:41],
                                        sc[0:1, 68:69]),
         waits=(("dve", d),))
    b.op("dve", lambda e: e.tensor_scalar_mul(sw[:, 0:1], sw[:, 0:1],
                                              1.0 - LAMBDA_INIT), inc="dve")
    d = b.v("dve")
    a_now = b.v("act")
    b.op("pe", lambda e: e.matmul(A[4].ap()[:, 32:33], p.c_ones_row.ap()[0:1, :],
                                  p.ln_small.ap()[0:1, 40:41],
                                  start=True, stop=True),
         waits=(("dve", d), ("act", a_now)), inc="pe")
    pm = b.v("pe")
    b.op("act", lambda e: e.copy(out=lnb[:, 0:1], in_=A[4].ap()[:, 32:33]),
         waits=(("pe", pm),), inc="act")
    return b.v("act")


def emit_l0_attention(p, qs, ks, vs, attn_c, masks, ebuf, negBs, lnb, sw,
                      b12s, osbuf, smalls, ready, slot_defs=None, qw=T,
                      acw=T, subln_defer=False):
    b = p.b
    A = p.banks
    q3 = qs.rearrange("p (c t) -> p c t", c=NTILE)
    k3 = ks.rearrange("p (c t) -> p c t", c=8)
    v3 = vs.rearrange("p (c t) -> p c t", c=NTILE)
    m3 = masks.rearrange("p (c t) -> p c t", t=512)
    e3 = ebuf.rearrange("p (c t) -> p c t", c=4)
    ac3 = attn_c.rearrange("p (c t) -> p c t", c=NTILE)
    sm = smalls
    if slot_defs is None:
        slot_defs = [
            (0, 0, SLOT_KTS[0], {kt: MASKED.get((0, kt)) for kt in SLOT_KTS[0]}),
            (512, 512, SLOT_KTS[1], {kt: MASKED.get((1, kt)) for kt in SLOT_KTS[1]}),
        ]
    first = True
    prev_evac_dve = 0
    prev_recip_dve = 0
    prev_exp_act = 0
    hook = getattr(p, "dbg_attn_hook", None)
    hookh = p.c_h128.ap() if hook is not None else None
    for h in range(16):
        vh = h // 2
        if hook is not None and h in hook:
            b.op("sync", lambda e, h=h: e.dma_start(
                out=hook[h], in_=hookh[:, :]),
                waits=(("dve", b.v("dve")), ("act", b.v("act")),
                       ("pe", b.v("pe"))), inc="dbg", n=16)
            wa = getattr(p, "dbg_win_a1", None)
            if wa is not None and h == 1:
                b.op("sync", lambda e, wa=wa: e.dma_start(
                    out=wa[0], in_=wa[1].ap()[:, :]), inc="dbg", n=16)
        for qcb, acb, kts, mmap in slot_defs:
            n = len(kts)
            exp_m = {}
            pvm = {}
            maskd = {}
            for i, kt in enumerate(kts):
                par = i % 2
                bS1 = A[4 + par]; bS2 = A[6]
                w = []
                if first:
                    w += ready
                    first = False
                if i >= 2:
                    w.append(("act", exp_m[i - 2]))
                if i == 0 and subln_defer and prev_exp_act:
                    w.append(("act", prev_exp_act))
                if i == 0 and not subln_defer and prev_evac_dve:
                    w.append(("dve", prev_evac_dve))
                kb = 64 * (h % 2)
                t1, b1_ = qpos(2 * h)
                t2, b2_ = qpos(2 * h + 1)
                assert b1_ == kb and b2_ == kb
                b.op("pe", lambda e, h=h, qcb=qcb, kt=kt, bS1=bS1, kb=kb,
                     t1=t1: e.matmul(
                    bS1.ap()[:, :],
                    k3[:, h // 2, kt * 128:(kt + 1) * 128][kb:kb + 64, :],
                    q3[:, t1, qcb:qcb + 512][kb:kb + 64, :],
                    start=True, stop=True), waits=w)
                w2 = []
                if i >= 1:
                    w2.append(("act", exp_m[i - 1]))
                b.op("pe", lambda e, h=h, qcb=qcb, kt=kt, bS2=bS2, kb=kb,
                     t2=t2: e.matmul(
                    bS2.ap()[:, :],
                    k3[:, h // 2, kt * 128:(kt + 1) * 128][kb:kb + 64, :],
                    q3[:, t2, qcb:qcb + 512][kb:kb + 64, :],
                    start=True, stop=True), waits=w2, inc="pe")
                pm = b.v("pe")
                wa = [("pe", pm)]
                if i >= 2:
                    wa.append(("pe", pvm[i - 2]))
                b.op("act", lambda e, h=h, par=par, bS1=bS1: e.activation(
                    out=e3[:, par, :], in_=bS1.ap()[:, :], func=AF.Exp,
                    bias=negBs.ap()[:, 2 * h:2 * h + 1], scale=0.125), waits=wa)
                b.op("act", lambda e, h=h, par=par, bS2=bS2: e.activation(
                    out=e3[:, 2 + par, :], in_=bS2.ap()[:, :], func=AF.Exp,
                    bias=negBs.ap()[:, 2 * h + 1:2 * h + 2], scale=0.125),
                    inc="act")
                exp_m[i] = b.v("act")
                mi = mmap.get(kt)
                if mi is not None:
                    b.op("dve", lambda e, par=par, mi=mi: e.tensor_mul(
                        e3[:, par, :], e3[:, par, :], m3[:, mi, :]),
                        waits=(("act", exp_m[i]),))
                    b.op("dve", lambda e, par=par, mi=mi: e.tensor_mul(
                        e3[:, 2 + par, :], e3[:, 2 + par, :], m3[:, mi, :]),
                        inc="dve")
                    maskd[i] = ("dve", b.v("dve"))
                else:
                    maskd[i] = ("act", exp_m[i])
                w_sum = [maskd[i]]
                if i == 0 and subln_defer and prev_recip_dve:
                    w_sum.append(("dve", prev_recip_dve))
                b.op("pe", lambda e, par=par, i=i, n=n: e.matmul(
                    A[2].ap()[0:1, :], p.c_ones_col.ap()[:, 0:1], e3[:, par, :],
                    start=(i == 0), stop=(i == n - 1)), waits=w_sum)
                b.op("pe", lambda e, par=par, i=i, n=n: e.matmul(
                    A[3].ap()[0:1, :], p.c_ones_col.ap()[:, 0:1],
                    e3[:, 2 + par, :], start=(i == 0), stop=(i == n - 1)))
                w_pv = []
                if i == 0 and subln_defer and prev_evac_dve:
                    w_pv.append(("dve", prev_evac_dve))
                b.op("pe", lambda e, kt=kt, vh=vh, par=par, i=i, n=n: e.matmul(
                    A[0].ap()[:, :], v3[:, kt, vh * 128:(vh + 1) * 128],
                    e3[:, par, :], start=(i == 0), stop=(i == n - 1)),
                    waits=w_pv)
                b.op("pe", lambda e, kt=kt, vh=vh, par=par, i=i, n=n: e.matmul(
                    A[1].ap()[:, :], v3[:, kt, vh * 128:(vh + 1) * 128],
                    e3[:, 2 + par, :], start=(i == 0), stop=(i == n - 1)),
                    inc="pe")
                pvm[i] = b.v("pe")
            last_pe = pvm[n - 1]
            prev_exp_act = b.v("act")
            b.op("dve", lambda e: e.reciprocal(sm[0:1, 0:512], A[2].ap()[0:1, :]),
                 waits=(("pe", last_pe),))
            b.op("dve", lambda e: e.reciprocal(sm[0:1, 512:1024],
                                               A[3].ap()[0:1, :]), inc="dve")
            prev_recip_dve = b.v("dve")
            dm = b.v("dve")
            b.op("dve", lambda e: e.tensor_scalar_min(
                p.ln_small.ap()[0:1, 1024:1536], sm[0:1, 0:512], 1e30),
                 waits=(("dve", dm),))
            b.op("dve", lambda e: e.tensor_scalar_min(
                p.ln_small.ap()[0:1, 1536:2048], sm[0:1, 512:1024], 1e30),
                 inc="dve")
            dm = b.v("dve")
            b.op("pe", lambda e: e.matmul(A[7].ap()[:, :], p.c_ones_row.ap()[0:1, :],
                                          p.ln_small.ap()[0:1, 1024:1536],
                                          start=True, stop=True),
                 waits=(("dve", dm),), inc="pe")
            pm = b.v("pe")
            b.op("act", lambda e: e.copy(out=b12s[:, 0:512], in_=A[7].ap()[:, :]),
                 waits=(("pe", pm),), inc="act")
            am = b.v("act")
            b.op("pe", lambda e: e.matmul(A[7].ap()[:, :], p.c_ones_row.ap()[0:1, :],
                                          p.ln_small.ap()[0:1, 1536:2048],
                                          start=True, stop=True),
                 waits=(("act", am),), inc="pe")
            pm = b.v("pe")
            b.op("act", lambda e: e.copy(out=b12s[:, 512:1024],
                                         in_=A[7].ap()[:, :]),
                 waits=(("pe", pm),), inc="act")
            am = b.v("act")
            b.op("dve", lambda e: e.scalar_tensor_tensor(
                osbuf[:, 512:1024], A[1].ap()[:, :], lnb[:, 0:1],
                b12s[:, 512:1024], ALU.mult, ALU.mult),
                waits=(("act", am),))
            b.op("dve", lambda e: e.scalar_tensor_tensor(
                osbuf[:, 0:512], A[0].ap()[:, :], 1.0,
                b12s[:, 0:512], ALU.mult, ALU.mult), inc="dve")
            dmw = b.v("dve")
            if subln_defer:
                b.op("dve", lambda e, h=h, acb=acb: e.tensor_add(
                    ac3[:, h, acb:acb + 512], osbuf[:, 0:512],
                    osbuf[:, 512:1024]), waits=(("dve", dmw),), inc="dve")
                prev_evac_dve = b.v("dve")
                continue
            b.op("dve", lambda e: e.tensor_add(
                osbuf[:, 0:512], osbuf[:, 0:512],
                osbuf[:, 512:1024]), waits=(("dve", dmw),), inc="dve")
            prev_evac_dve = b.v("dve")
            b.op("act", lambda e: e.activation(
                out=b12s[:, 0:512], in_=osbuf[:, 0:512],
                func=AF.Square), waits=(("dve", prev_evac_dve),), inc="act")
            am = b.v("act")
            b.op("pe", lambda e: e.matmul(
                A[7].ap()[0:1, :], p.c_ones_col.ap()[:, 0:1], b12s[:, 0:512],
                start=True, stop=True), waits=(("act", am),), inc="pe")
            pm = b.v("pe")
            b.op("dve", lambda e: e.tensor_scalar(
                sm[0:1, 0:512], A[7].ap()[0:1, :], 1.0 / 128.0, 1e-5,
                ALU.mult, ALU.add), waits=(("pe", pm),), inc="dve")
            dm = b.v("dve")
            b.op("act", lambda e: e.activation(out=sm[0:1, 0:512],
                                               in_=sm[0:1, 0:512], func=AF.Sqrt),
                 waits=(("dve", dm),), inc="act")
            am = b.v("act")
            b.op("dve", lambda e: e.reciprocal(sm[0:1, 512:1024], sm[0:1, 0:512]),
                 waits=(("act", am),), inc="dve")
            dm = b.v("dve")
            b.op("dve", lambda e: e.tensor_copy(p.ln_small.ap()[0:1, 1024:1536],
                                                sm[0:1, 512:1024]),
                 waits=(("dve", dm),), inc="dve")
            dm = b.v("dve")
            b.op("pe", lambda e: e.matmul(A[7].ap()[:, :], p.c_ones_row.ap()[0:1, :],
                                          p.ln_small.ap()[0:1, 1024:1536],
                                          start=True, stop=True),
                 waits=(("dve", dm),), inc="pe")
            pm = b.v("pe")
            b.op("dve", lambda e, h=h, acb=acb: e.scalar_tensor_tensor(
                ac3[:, h, acb:acb + 512],
                osbuf[:, 0:512], sw[:, 0:1], A[7].ap()[:, :],
                ALU.mult, ALU.mult), waits=(("pe", pm),), inc="dve")
            prev_evac_dve = b.v("dve")
    return b.v("dve")


def emit_outproj_residual(p, w_dram, src_slab, x_dram, x1_arena, spill_dram,
                          wbuf, xold, src_ready, dout=D, lowrank=None,
                          lr=None):
    b = p.b
    ndt = dout // 128
    s3 = src_slab.rearrange("p (c t) -> p c t", c=NTILE)
    x13 = x1_arena
    wb3 = wbuf.ap()[:, 0:2 * NTILE * 128].rearrange("p (c n) -> p c n",
                                                    c=2 * NTILE)
    xo3 = xold.rearrange("p (c t) -> p c t", c=2)
    A = p.banks
    evac = {}
    entry_act = b.v("act")
    entry_pe = b.v("pe")
    entry_dve = b.v("dve")
    for dt in range(ndt):
        buf = dt % 2
        wsem = "o_w%d" % buf
        xsem = "o_x%d" % buf
        w = [("pe", entry_pe)] if dt < 2 else [("pe", evac[dt - 2][0])]
        b.op("pool", lambda e, dt=dt, buf=buf: e.dma_start(
            out=wb3[:, buf * NTILE:(buf + 1) * NTILE, :],
            in_=chunked_cols(w_dram, dt * 128, 128)),
            waits=w, inc=wsem, n=16)
        wv = b.v(wsem)
        wx = [("dve", evac[dt - 2][1])] if dt >= 2 else []
        b.op("sync", lambda e, dt=dt, buf=buf: e.dma_start(
            out=xo3[:, buf, :], in_=x_dram.rearrange(
                "(c p) t -> c p t", p=128)[dt, :, :]),
            waits=wx, inc=xsem, n=16)
        xv = b.v(xsem)
        pe_w = [(wsem, wv)]
        if dt == 0 and src_ready:
            pe_w.append(src_ready)
        if dt < 2:
            pe_w.append(("act", entry_act))
            pe_w.append(("dve", entry_dve))
        if dt >= 2:
            pe_w.append(("dve", evac[dt - 2][1]))
        for tch in range(2):
            bank = A[2 * buf + tch]
            for kc in range(NTILE):
                b.op("pe", lambda e, buf=buf, tch=tch, kc=kc, bank=bank: e.matmul(
                    bank.ap()[:, :], wb3[:, buf * NTILE + kc, :],
                    s3[:, kc, tch * 512:(tch + 1) * 512],
                    start=(kc == 0),
                    stop=(lr is None and kc == NTILE - 1)),
                    waits=(pe_w + ([("act", lr[2])] if (lr and dt == 0) else []))
                    if (tch == 0 and kc == 0) else (),
                    inc=None if lr is not None and (tch == 1 and kc == NTILE - 1)
                    else ("pe" if (lr is None and tch == 1 and kc == NTILE - 1)
                          else None))
            if lr is not None:
                b.op("pe", lambda e, dt=dt, tch=tch, bank=bank: e.matmul(
                    bank.ap()[:, :], lr[0][0:16, dt * 128:(dt + 1) * 128],
                    lr[1][0:16, tch * 512:(tch + 1) * 512],
                    start=False, stop=True),
                    inc="pe" if tch == 1 else None)
        if lowrank is not None:
            lowrank(dt)
        pm = b.v("pe")
        for tch in range(2):
            bank = A[2 * buf + tch]
            b.op("dve", lambda e, dt=dt, tch=tch, buf=buf, bank=bank: e.tensor_add(
                x13[:, dt, tch * 512:(tch + 1) * 512], bank.ap()[:, :],
                xo3[:, buf, tch * 512:(tch + 1) * 512]),
                waits=((("pe", pm), (xsem, xv)) if tch == 0 else ()),
                inc="dve" if tch == 1 else None)
        dm = b.v("dve")
        evac[dt] = (pm, dm)
        b.op("sync", lambda e, dt=dt: e.dma_start(
            out=spill_dram.rearrange("(c p) t -> c p t", p=128)[dt, :, :],
            in_=x13[:, dt, :]), waits=(("dve", dm),), inc="spill", n=16)
    return b.v("dve"), b.v("spill")


def emit_ffn0(p, w1T, w3T, w2T, xn, hidden, wbuf, w2buf, x1_dram, x2_dram,
              xold, x2ring, scratch_f32, xn_ready, spill_cnt,
              wscale_up=None, wscale_gate=None, sudp=None, out_scale=SQ8192,
              wscale_dp=None, silu_buf=None, dbg=None):
    b = p.b
    A = p.banks
    xn3 = xn.rearrange("p (c t) -> p c t", c=NTILE)
    h3 = hidden.rearrange("p (c t) -> p c t", c=64)
    wb3 = wbuf.ap()[:, 0:4 * NTILE * 128].rearrange("p (c n) -> p c n",
                                                     c=4 * NTILE)
    w2b3 = w2buf.rearrange("p (c n) -> p c n", c=2 * 32)
    si3 = silu_buf.rearrange("p (c t) -> p c t", c=2)
    sc = scratch_f32
    evac = {}
    entry_pe = b.v("pe")
    entry_act = b.v("act")
    entry_dve = b.v("dve")
    for hs in range(64):
        buf = hs % 2
        fsem = "f_w%d" % buf
        w = [("pe", entry_pe)] if hs < 2 else [("pe", evac[hs - 2][0])]
        b.op("pool", lambda e, hs=hs, buf=buf: e.dma_start(
            out=wb3[:, buf * NTILE:(buf + 1) * NTILE, :],
            in_=chunked_cols(w1T, hs * 128, 128)),
            waits=w, inc=fsem, n=16)
        b.op("pool", lambda e, hs=hs, buf=buf: e.dma_start(
            out=wb3[:, (2 + buf) * NTILE:(3 + buf) * NTILE, :],
            in_=chunked_cols(w3T, hs * 128, 128)),
            inc=fsem, n=16)
        wv = b.v(fsem)
        pe_w = [(fsem, wv)]
        if hs == 0 and xn_ready:
            pe_w.append(xn_ready)
        if hs < 2:
            pe_w.append(("act", entry_act))
            pe_w.append(("dve", entry_dve))
        if hs >= 2:
            pe_w.append(("dve", evac[hs - 2][1]))
        bb = 4 * buf
        for tch in range(2):
            for kc in range(NTILE):
                b.op("pe", lambda e, buf=buf, tch=tch, kc=kc, bb=bb: e.matmul(
                    A[bb + tch].ap()[:, :], wb3[:, buf * NTILE + kc, :],
                    xn3[:, kc, tch * 512:(tch + 1) * 512],
                    start=(kc == 0), stop=(kc == NTILE - 1)),
                    waits=pe_w if (tch == 0 and kc == 0) else ())
            for kc in range(NTILE):
                b.op("pe", lambda e, buf=buf, tch=tch, kc=kc, bb=bb: e.matmul(
                    A[bb + 2 + tch].ap()[:, :], wb3[:, (2 + buf) * NTILE + kc, :],
                    xn3[:, kc, tch * 512:(tch + 1) * 512],
                    start=(kc == 0), stop=(kc == NTILE - 1)),
                    inc="pe" if (tch == 1 and kc == NTILE - 1) else None)
        pm = b.v("pe")
        for tch in range(2):
            if wscale_gate is not None:
                b.op("act", lambda e, tch=tch, buf=buf, hs=hs, bb=bb: e.activation(
                    out=si3[:, tch, :], in_=A[bb + tch].ap()[:, :], func=AF.Silu,
                    scale=wscale_gate.ap()[:, hs:hs + 1]),
                    waits=((("pe", pm),) if tch == 0 else ()),
                    inc="act" if tch == 1 else None)
            else:
                b.op("act", lambda e, tch=tch, buf=buf, bb=bb: e.activation(
                    out=si3[:, tch, :], in_=A[bb + tch].ap()[:, :], func=AF.Silu),
                    waits=((("pe", pm),) if tch == 0 else ()),
                    inc="act" if tch == 1 else None)
        am = b.v("act")
        for tch in range(2):
            wdve = [("act", am)] if tch == 0 else []
            if hs == 0 and tch == 0 and spill_cnt:
                wdve.append(("spill", spill_cnt))
            if wscale_up is not None:
                b.op("dve", lambda e, tch=tch, hs=hs, bb=bb: e.scalar_tensor_tensor(
                    h3[:, hs, tch * 512:(tch + 1) * 512],
                    A[bb + 2 + tch].ap()[:, :], wscale_up.ap()[:, hs:hs + 1],
                    si3[:, tch, :], ALU.mult, ALU.mult),
                    waits=wdve, inc="dve" if tch == 1 else None)
            else:
                b.op("dve", lambda e, tch=tch, hs=hs, bb=bb: e.scalar_tensor_tensor(
                    h3[:, hs, tch * 512:(tch + 1) * 512],
                    A[bb + 2 + tch].ap()[:, :], 1.0,
                    si3[:, tch, :], ALU.mult, ALU.mult),
                    waits=wdve, inc="dve" if tch == 1 else None)
        dm = b.v("dve")
        evac[hs] = (pm, dm)
    dm_all = b.v("dve")
    if dbg is not None:
        b.op("sync", lambda e: e.dma_start(
            out=dbg["hid_pre"].rearrange("(c p) t -> p c t", p=128),
            in_=h3), waits=(("dve", dm_all),), inc="dbgs", n=16)
        dv = b.v("dbgs")
        b.op("dve", lambda e: None, waits=(("dbgs", dv),))
        b.op("dve", lambda e: e.engine_nop(), inc="dve")
        dm_all = b.v("dve")
    if sudp is not None:
        for hc in range(64):
            b.op("dve", lambda e, hc=hc: e.tensor_scalar_mul(
                h3[:, hc, :], h3[:, hc, :], sudp.ap()[:, hc:hc + 1]),
                inc="dve" if hc == 63 else None)
        dm_all = b.v("dve")
    fw_done = emit_fwht(p, h3, sc, 64, dm_all)
    if dbg is not None:
        b.op("sync", lambda e: e.dma_start(
            out=dbg["hid_post"].rearrange("(c p) t -> p c t", p=128),
            in_=h3), waits=(("dve", fw_done),), inc="dbgs", n=16)
        dv = b.v("dbgs")
        b.op("dve", lambda e: None, waits=(("dbgs", dv),))
        b.op("dve", lambda e: e.engine_nop(), inc="dve")
        fw_done = b.v("dve")
    entry_pe = b.v("pe")
    prev_dve = {}
    prev_pe_group = {}
    osem_war = {}
    for dt in range(NTILE):
        for halfh in range(2):
            slot = (2 * dt + halfh) % 2
            w = [("pe", entry_pe), ("dve", fw_done)] if (2 * dt + halfh) < 2 \
                else [("pe", prev_pe_group[(2 * dt + halfh) - 2])]
            f2sem = "f2_w%d" % slot
            b.op("pool", lambda e, dt=dt, halfh=halfh, slot=slot: e.dma_start(
                out=w2b3[:, slot * 32:(slot + 1) * 32, :],
                in_=w2T[halfh * 4096:(halfh + 1) * 4096,
                        dt * 128:(dt + 1) * 128].rearrange(
                            "(c p) n -> p c n", p=128)),
                waits=w, inc=f2sem, n=16)
            wv = b.v(f2sem)
            bsel = 2 * (dt % 2)
            pe_w = [(f2sem, wv)]
            if dt < 2 and halfh == 0:
                pe_w.append(("dve", fw_done))
            if dt >= 2 and halfh == 0:
                pe_w.append(("dve", prev_dve[dt - 2]))
            for tch in range(2):
                for hc in range(32):
                    b.op("pe", lambda e, dt=dt, halfh=halfh, slot=slot, tch=tch,
                         hc=hc, bsel=bsel: e.matmul(
                        A[bsel + tch].ap()[:, :], w2b3[:, slot * 32 + hc, :],
                        h3[:, halfh * 32 + hc, tch * 512:(tch + 1) * 512],
                        start=(halfh == 0 and hc == 0),
                        stop=(halfh == 1 and hc == 31)),
                        waits=pe_w if (tch == 0 and hc == 0) else (),
                        inc="pe" if (tch == 1 and hc == 31) else None)
            prev_pe_group[2 * dt + halfh] = b.v("pe")
        pm = b.v("pe")
        wx = [("dve", prev_dve[dt - 2])] if dt >= 2 else [("dve", fw_done)]
        fxsem = "fx_%d" % (dt % 2)
        b.op("sync", lambda e, dt=dt: e.dma_start(
            out=xold.rearrange("p (c t) -> p c t", c=2)[:, dt % 2, :],
            in_=x1_dram.rearrange("(c p) t -> c p t", p=128)[dt, :, :]),
            waits=wx, inc=fxsem, n=16)
        xv = b.v(fxsem)
        for tch in range(2):
            bank = A[2 * (dt % 2) + tch]
            if wscale_dp is not None:
                b.op("dve", lambda e, dt=dt, tch=tch, bank=bank: e.scalar_tensor_tensor(
                    x2ring.rearrange("p (c t) -> p c t", c=2)[:, dt % 2, tch * 512:(tch + 1) * 512],
                    bank.ap()[:, :], wscale_dp.ap()[:, dt:dt + 1],
                    xold.rearrange("p (c t) -> p c t", c=2)[:, dt % 2, tch * 512:(tch + 1) * 512],
                    ALU.mult, ALU.add),
                    waits=((("pe", pm), (fxsem, xv)) +
                           ((("osem", osem_war[dt - 2]),) if dt >= 2 else ())
                           if tch == 0 else ()),
                    inc="dve" if tch == 1 else None)
            else:
                b.op("dve", lambda e, dt=dt, tch=tch, bank=bank: e.scalar_tensor_tensor(
                    x2ring.rearrange("p (c t) -> p c t", c=2)[:, dt % 2, tch * 512:(tch + 1) * 512],
                    bank.ap()[:, :], out_scale,
                    xold.rearrange("p (c t) -> p c t", c=2)[:, dt % 2, tch * 512:(tch + 1) * 512],
                    ALU.mult, ALU.add),
                    waits=((("pe", pm), (fxsem, xv)) +
                           ((("osem", osem_war[dt - 2]),) if dt >= 2 else ())
                           if tch == 0 else ()),
                    inc="dve" if tch == 1 else None)
        prev_dve[dt] = b.v("dve")
        b.op("sync", lambda e, dt=dt: e.dma_start(
            out=x2_dram.rearrange("(c p) t -> c p t", p=128)[dt, :, :],
            in_=x2ring.rearrange("p (c t) -> p c t", c=2)[:, dt % 2, :]),
            waits=(("dve", prev_dve[dt]),), inc="osem", n=16)
        osem_war[dt] = b.v("osem")
    return b.v("osem")


def emit_fwht(p, h3, sc, nch, hidden_ready):
    b = p.b
    A = p.banks
    h128ap = p.c_h128.ap()
    nst = {16: 4, 64: 6}[nch]
    for tb in range(8):
        prev_act = b.v("act")
        prev_dve = b.v("dve")
        for c4 in range(nch // 4):
            bank = A[6 + c4 % 2]
            w = []
            if c4 == 0:
                w.append(("dve", hidden_ready if tb == 0 else b.v("dve")))
            if c4 >= 2:
                w.append(("act", evacm[c4 - 2]))
            for j in range(4):
                c = 4 * c4 + j
                b.op("pe", lambda e, c=c, j=j, bank=bank, tb=tb: e.matmul(
                    bank.ap()[:, j * 128:(j + 1) * 128], h128ap[:, :],
                    h3[:, c, tb * 128:(tb + 1) * 128],
                    start=True, stop=True),
                    waits=w if j == 0 else (),
                    inc="pe" if j == 3 else None)
            pm = b.v("pe")
            b.op("act", lambda e, c4=c4, bank=bank: e.copy(
                out=sc[:, c4 * 512:(c4 + 1) * 512], in_=bank.ap()[:, :]),
                waits=(("pe", pm),), inc="act")
            if c4 == 0:
                evacm = {}
            evacm[c4] = b.v("act")
        am = b.v("act")
        dbgf = getattr(p, "dbg_fwht", None)
        if dbgf is not None and tb == 0:
            b.op("sync", lambda e, dbgf=dbgf: e.dma_start(out=dbgf[0], in_=sc),
                 waits=(("act", am),), inc="dbgf", n=16)
            dv = b.v("dbgf")
            b.op("act", lambda e: e.copy(out=p.ln_small.ap()[0:1, 0:4],
                                         in_=p.c_ones_row.ap()[0:1, 0:4]),
                 waits=(("dbgf", dv),), inc="act")
            am = b.v("act")
        first = True
        for st in range(nst):
            g = 1 << st
            na = nch // (2 * g)
            scv = sc.rearrange("p (a two b t) -> p a two b t", a=na, two=2, b=g)
            Av = scv[:, :, 0, :, :]
            Bv = scv[:, :, 1, :, :]
            w = [("act", am)] if first else [("dve", b.v("dve"))]
            first = False
            b.op("dve", lambda e, Av=Av, Bv=Bv: e.tensor_add(Av, Av, Bv),
                 waits=w, inc="dve")
            dmw = b.v("dve")
            b.op("dve", lambda e, Av=Av, Bv=Bv: e.scalar_tensor_tensor(
                Bv, Bv, -2.0, Av, ALU.mult, ALU.add),
                waits=(("dve", dmw),), inc="dve")
        nch_ = nch
        dmw = b.v("dve")
        if dbgf is not None and tb == 0:
            b.op("sync", lambda e, dbgf=dbgf: e.dma_start(out=dbgf[1], in_=sc),
                 waits=(("dve", dmw),), inc="dbgf", n=16)
            dv = b.v("dbgf")
            b.op("dve", lambda e: None, waits=(("dbgf", dv),))
            b.op("dve", lambda e: e.engine_nop(), inc="dve")
            dmw = b.v("dve")
            p.dbg_fwht = None
        b.op("dve", lambda e, tb=tb, nch_=nch_: e.tensor_copy(
            sb3_scatter(h3, tb),
            sc.rearrange("p (c t) -> p c t", c=nch_)),
            waits=(("dve", dmw),), inc="dve")
    return b.v("dve")


def sb3_scatter(h3, tb):
    return h3[:, :, tb * 128:(tb + 1) * 128]


def build_l0_rest(debug=False, p=None, io=None, permute_q=False):
    standalone = p is None
    if standalone:
        p = Prog("l0rest")
    io = io or {}
    def din(name, shape, dtype):
        return io[name] if name in io else p.din(name, shape, dtype)
    def dout(name, shape, dtype):
        return io[name] if name in io else p.dout(name, shape, dtype)
    b = p.b
    x_in = din("x", [D, T], F32)
    q_in = din("q", [D, T], BF16)
    k_in = din("kfull", [1024, 2048], BF16)
    v_in = din("vfull", [2048, 1024], BF16)
    m_in = din("masks", [2048, 512], BF16)
    lam_in = din("lam", [1, 256], F32)
    subln_in = din("sublnw", [128, 1], F32)
    woT = din("woT", [D, D], F32)
    w1T = din("w1T", [D, 8192], F32)
    w3T = din("w3T", [D, 8192], F32)
    w2T = din("w2T", [8192, D], F32)
    x2_out = dout("x2", [D, T], F32)
    x1_out = dout("x1", [D, T], F32)
    ac_out = dout("attn_c", [D, T], BF16) if debug else None

    arena = p.sb("arena", [128, 90112], BF16)
    wbuf = p.sb("wbuf", [128, 8192], BF16)
    if standalone:
        p.make_common(rope=False)
    p.c_h128 = p.sb("h128", [128, 128], BF16)
    h128_in = din("h128_in", [128, 128], BF16)
    negBs = p.sb("negBs", [128, 64], F32)
    lnb_t = p.sb("lnb", [128, 1], F32)
    sw_t = p.sb("sw", [128, 1], F32)
    smax = p.sb("smax", [1, 256], F32)
    lsc_t = p.sb("lsc", [1, 256], F32)

    ar = arena.ap()
    arf = ar.bitcast(F32)
    qs = ar[:, 0:16384]
    ks = ar[:, 16384:32768]
    vs = ar[:, 32768:49152]
    attn_c = ar[:, 49152:65536]
    masks = ar[:, 65536:73728]
    ebuf = ar[:, 73728:75776]
    b12s = ar[:, 75776:76800]
    silu_b = ar[:, 81920:82944]
    osbuf = arf[:, 38400:39424]
    smalls_ap = arf[0:1, 39424:40448]
    x1_arena = arf[:, 0:16384].rearrange("p (c t) -> p c t", c=NTILE)
    xn2 = ar[:, 65536:81920]
    rbuf2 = ar[:, 55296:57344]
    stats2 = arf[0:1, 24576:27648]
    hidden = ar[:, 0:65536]
    fwht_sc = arf[:, 32768:40960]
    x2ring = arf[:, 40960:43008]
    xold = arf[:, 43008:45056]
    w2buf = ar[:, 65536:73728]
    b.op("sync", lambda e: e.dma_start(
        out=sb3_like(qs, NTILE), in_=q_in.rearrange("(c p) t -> p c t", p=128)),
        inc="adma", n=16)
    b.op("sync", lambda e: e.dma_start(
        out=sb3_like(ks, 8), in_=k_in.rearrange("(c p) t -> p c t", p=128)),
        inc="adma", n=16)
    b.op("sync", lambda e: e.dma_start(
        out=sb3_like(vs, NTILE), in_=v_in.rearrange("(c p) t -> p c t", p=128)),
        inc="adma", n=16)
    b.op("sync", lambda e: e.dma_start(
        out=masks.rearrange("p (c n) -> p c n", c=16),
        in_=m_in.rearrange("(c p) t -> p c t", p=128)),
        inc="adma", n=16)
    b.op("sync", lambda e: e.dma_start(out=p.c_h128.ap()[:, :], in_=h128_in),
        inc="adma", n=16)
    ad = b.v("adma")

    lam_m = emit_l0_lambda(p, lam_in, subln_in, lsc_t.ap(), lnb_t.ap(),
                           sw_t.ap())
    bounds_m = emit_l0_bounds(p, qs, ks, attn_c[:, 0:2048], smax, negBs,
                              ready=[("adma", ad)])
    ready = [("adma", ad), ("act", max(lam_m, bounds_m))]
    attn_m = emit_l0_attention(p, qs, ks, vs, attn_c, masks, ebuf, negBs,
                               lnb_t.ap(), sw_t.ap(), b12s, osbuf,
                               smalls_ap, ready)
    if debug:
        b.op("sync", lambda e: e.dma_start(
            out=ac_out.rearrange("(c p) t -> p c t", p=128),
            in_=sb3_like(attn_c, NTILE)),
            waits=(("dve", attn_m),), inc="spill", n=16)
        nb_out = dout("negBs_dbg", [128, 64], F32)
        sm_out = dout("smax_dbg", [1, 256], F32)
        ls_out = dout("lnb_dbg", [128, 1], F32)
        ls_out2 = dout("sw_dbg", [128, 1], F32)
        b.op("sync", lambda e: e.dma_start(out=nb_out, in_=negBs.ap()[:, :]),
             waits=(("dve", attn_m),), inc="spill", n=16)
        b.op("sync", lambda e: e.dma_start(out=sm_out, in_=smax.ap()[:, :]),
             waits=(("dve", attn_m),), inc="spill", n=16)
        b.op("sync", lambda e: e.dma_start(out=ls_out, in_=lnb_t.ap()[:, :]),
             waits=(("dve", attn_m),), inc="spill", n=16)
        b.op("sync", lambda e: e.dma_start(out=ls_out2, in_=sw_t.ap()[:, :]),
             waits=(("dve", attn_m),), inc="spill", n=16)

    op_m, spill_cnt = emit_outproj_residual(
        p, woT, attn_c, x_in, x1_arena, x1_out, wbuf, xold, ("dve", attn_m))

    if debug:
        p.dbg_sq = dout("sq_dbg", [D, T], BF16)
        p.dbg_xt = dout("xt_dbg", [D, T], F32)
    ln2_m = emit_layernorm(p, x1_arena,
                           xn2.rearrange("p (c t) -> p c t", c=NTILE),
                           stats2, rbuf2,
                           entry=(("dve", op_m),))

    spill_all = b.v("spill")
    dbg = None
    if debug:
        dbg = {"hid_pre": p.dout("hid_pre", [8192, T], BF16),
               "hid_post": p.dout("hid_post", [8192, T], BF16)}
        xn2_out = dout("xn2_dbg", [D, T], BF16)
        st_out = dout("stats_dbg", [1, 3072], F32)
        lns_out = dout("lns_dbg", [1, 2048], F32)
        b.op("sync", lambda e: e.dma_start(out=st_out, in_=stats2),
             waits=(("dve", ln2_m),), inc="spill", n=16)
        lns_f = arf[0:1, 28672:30720]
        b.op("dve", lambda e: e.tensor_copy(lns_f, p.ln_small.ap()[:, :]),
             waits=(("dve", ln2_m),), inc="dve")
        _d = b.v("dve")
        b.op("sync", lambda e: e.dma_start(out=lns_out, in_=lns_f),
             waits=(("dve", _d),), inc="spill", n=16)
        b.op("sync", lambda e: e.dma_start(
            out=xn2_out.rearrange("(c p) t -> p c t", p=128),
            in_=xn2.rearrange("p (c t) -> p c t", c=NTILE)),
            waits=(("dve", ln2_m),), inc="spill", n=16)
        spill_all = b.v("spill")
    o_cnt = emit_ffn0(p, w1T, w3T, w2T, xn2, hidden, wbuf, w2buf,
                      x1_out, x2_out, xold, x2ring, fwht_sc,
                      ("dve", ln2_m), spill_all, silu_buf=silu_b, dbg=dbg)

    b.op("sync", lambda e: None,
         waits=(("osem", o_cnt), ("spill", spill_all)))
    return p


def sb3_like(ap2, nch):
    return ap2.rearrange("p (c t) -> p c t", c=nch)


def emit_lowrank_u(p, a_dram, xn, u_sb, xn_ready, sem_prefix, bank,
                   is_view=False):
    b = p.b
    A = p.banks
    xn3 = xn if is_view else xn.rearrange("p (c t) -> p c t", c=NTILE)
    wsem = sem_prefix + "_wdma"
    entry_pe = b.v("pe")
    lr3 = p.lrbuf.ap().rearrange("p (c n) -> p c n", c=NTILE)
    b.op("pool", lambda e: e.dma_start(
        out=lr3, in_=a_dram.rearrange("(c p) n -> p c n", p=128)),
        waits=((("pe", entry_pe),) if entry_pe else ()), inc=wsem, n=16)
    wv = b.v(wsem)
    entry_act = b.v("act")
    xr = [xn_ready] if xn_ready else []
    for tch in range(2):
        base = 64 * tch
        for kc in range(NTILE):
            b.op("pe", lambda e, tch=tch, kc=kc, base=base: e.matmul(
                bank.ap()[base:base + 16, :],
                lr3[:, kc, :],
                xn3[:, kc, tch * 512:(tch + 1) * 512],
                start=(kc == 0), stop=(kc == NTILE - 1)),
                waits=([(wsem, wv), ("act", entry_act)] + xr
                       if (tch == 0 and kc == 0) else ()),
                inc="pe" if (tch == 1 and kc == NTILE - 1) else None)
    pm = b.v("pe")
    b.op("act", lambda e: e.copy(out=u_sb[:, 0:512], in_=bank.ap()[0:16, :]),
         waits=(("pe", pm),))
    b.op("act", lambda e: e.copy(out=u_sb[:, 512:1024],
                                 in_=bank.ap()[64:80, :]), inc="act")
    return b.v("act")


def emit_l1_qkv(p, xn, wq, wk, wv, qA, qB, kA, kB, vA, vB, qgain_ap,
                qs, ks, vs_tm, vs_fm, vhat, uq, uk, uv, wbuf, vwbuf,
                xn_ready):
    b = p.b
    A = p.banks
    xn3 = xn.rearrange("p (c t) -> p c t", c=NTILE)
    wb3 = wbuf.ap()[:, 0:2 * NTILE * 128].rearrange("p (c n) -> p c n",
                                                    c=2 * NTILE)
    q3 = qs.rearrange("p (c t) -> p c t", c=NTILE)
    k3 = ks.rearrange("p (c t) -> p c t", c=4)
    vt3 = vs_tm.rearrange("p (c t) -> p c t", c=8)
    vf3 = vs_fm.rearrange("p (c t) -> p c t", c=4)
    vh3 = vhat.rearrange("p (c t) -> p c t", c=4)
    qt3 = sb3(p.qtmp, 2, T)
    sm = p.l1small
    um = emit_lowrank_u(p, qA, xn, uq, xn_ready, "uq", A[6])
    um = emit_lowrank_u(p, kA, xn, uk, None, "uk", A[6])
    um = emit_lowrank_u(p, vA, xn, uv, None, "uv", A[6])

    def head_norm_rope(dt, src_bank_pair, out3, gain_slice, evacs):
        buf = dt % 2
        pm = b.v("pe")
        for tch in range(2):
            b.op("act", lambda e, tch=tch, buf=buf: e.copy(
                out=qt3[:, buf, tch * 512:(tch + 1) * 512],
                in_=src_bank_pair[tch].ap()[:, :]),
                waits=([("pe", pm)] + evacs if tch == 0 else ()),
                inc="act" if tch == 1 else None)
        am = b.v("act")
        b.op("act", lambda e, buf=buf: e.activation(
            out=p.l1sq.rearrange("p (c t) -> p c t", c=2)[:, buf, :],
            in_=qt3[:, buf, :], func=AF.Square), inc="act")
        am2 = b.v("act")
        dw = b.v("dve")
        for tch in range(2):
            b.op("pe", lambda e, tch=tch, buf=buf: e.matmul(
                A[4 + tch].ap()[0:1, :], p.c_ones_col.ap()[:, 0:1],
                p.l1sq.rearrange("p (c t) -> p c t", c=2)[:, buf,
                                                          tch * 512:(tch + 1) * 512],
                start=True, stop=True),
                waits=((("act", am2), ("dve", dw)) if tch == 0 else ()),
                inc="pe" if tch == 1 else None)
        pm2 = b.v("pe")
        for tch in range(2):
            b.op("dve", lambda e, tch=tch: e.tensor_scalar(
                sm[0:1, tch * 512:(tch + 1) * 512], A[4 + tch].ap()[0:1, :],
                1.0 / 128.0, 1.1920929e-07, ALU.mult, ALU.add),
                waits=((("pe", pm2),) if tch == 0 else ()),
                inc="dve" if tch == 1 else None)
        dm = b.v("dve")
        b.op("act", lambda e: e.activation(out=sm[0:1, 0:1024],
                                           in_=sm[0:1, 0:1024], func=AF.Sqrt),
             waits=(("dve", dm),), inc="act")
        am3 = b.v("act")
        b.op("dve", lambda e: e.reciprocal(sm[0:1, 0:1024], sm[0:1, 0:1024]),
             waits=(("act", am3),), inc="dve")
        dm = b.v("dve")
        if gain_slice is not None:
            b.op("dve", lambda e: e.tensor_scalar(
                p.ln_small.ap()[0:1, 0:1024], sm[0:1, 0:1024], gain_slice,
                None, ALU.mult), waits=(("dve", dm),), inc="dve")
        else:
            b.op("dve", lambda e: e.tensor_copy(p.ln_small.ap()[0:1, 0:1024],
                                                sm[0:1, 0:1024]),
                 waits=(("dve", dm),), inc="dve")
        dm = b.v("dve")
        for tch in range(2):
            b.op("pe", lambda e, tch=tch: e.matmul(
                A[6 + tch].ap()[:, :], p.c_ones_row.ap()[0:1, :],
                p.ln_small.ap()[0:1, tch * 512:(tch + 1) * 512],
                start=True, stop=True),
                waits=((("dve", dm),) if tch == 0 else ()),
                inc="pe" if tch == 1 else None)
        pm3 = b.v("pe")
        for tch in range(2):
            b.op("dve", lambda e, tch=tch, buf=buf, dt=dt: e.tensor_mul(
                out3[:, dt, tch * 512:(tch + 1) * 512][64:128, :],
                qt3[:, buf, tch * 512:(tch + 1) * 512][64:128, :],
                A[6 + tch].ap()[64:128, :]),
                waits=((("pe", pm3),) if tch == 0 else ()))
            b.op("dve", lambda e, tch=tch, buf=buf: e.tensor_mul(
                p.l1pre.rearrange("p (c t) -> p c t", c=2)[:, buf,
                                                           tch * 512:(tch + 1) * 512][0:64, :],
                qt3[:, buf, tch * 512:(tch + 1) * 512][0:64, :],
                A[6 + tch].ap()[0:64, :]),
                inc="dve" if tch == 1 else None)
        src = p.l1pre.rearrange("p (c t) -> p c t", c=2)[:, buf, :]
        dst = out3[:, dt, :]
        C = p.c_rope1c.ap(); SS = p.c_rope1s.ap()
        tmp = p.ropetmp.ap()
        dmw = b.v("dve")
        b.op("dve", lambda e: e.tensor_mul(dst[0:32, :], src[0:32, :],
                                           C[0:32, :]), waits=(("dve", dmw),))
        b.op("dve", lambda e: e.tensor_mul(tmp[0:32, :], src[32:64, :],
                                           SS[32:64, :]))
        b.op("dve", lambda e: e.tensor_mul(dst[32:64, :], src[32:64, :],
                                           C[32:64, :]))
        b.op("dve", lambda e: e.tensor_mul(tmp[32:64, :], src[0:32, :],
                                           SS[0:32, :]), inc="dve")
        dmw = b.v("dve")
        b.op("dve", lambda e: e.tensor_add(dst[0:32, :], dst[0:32, :],
                                           tmp[0:32, :]),
             waits=(("dve", dmw),))
        b.op("dve", lambda e: e.tensor_add(dst[32:64, :], dst[32:64, :],
                                           tmp[32:64, :]), inc="dve")
        return b.v("dve")

    def proj_heads(w_dram, lrB, u_sb, out3, ndt, gain, sem_prefix):
        evac = {}
        entry_pe = b.v("pe")
        for dt in range(ndt):
            buf = dt % 2
            psem = "%s_w%d" % (sem_prefix, buf)
            w = [("pe", entry_pe)] if dt < 2 else [("pe", evac[dt - 2][0])]
            b.op("pool", lambda e, dt=dt, buf=buf: e.dma_start(
                out=wb3[:, buf * NTILE:(buf + 1) * NTILE, :],
                in_=chunked_cols(w_dram, dt * 128, 128)),
                waits=w, inc=psem, n=16)
            wv_ = b.v(psem)
            pe_w = [(psem, wv_)]
            if dt >= 2:
                pe_w.append(("act", evac[dt - 2][1]))
            for tch in range(2):
                bank = A[2 * buf + tch]
                for kc in range(NTILE):
                    b.op("pe", lambda e, buf=buf, tch=tch, kc=kc, bank=bank: e.matmul(
                        bank.ap()[:, :], wb3[:, buf * NTILE + kc, :],
                        xn3[:, kc, tch * 512:(tch + 1) * 512],
                        start=(kc == 0), stop=False),
                        waits=pe_w if (tch == 0 and kc == 0) else ())
                b.op("pe", lambda e, dt=dt, tch=tch, bank=bank, lrB=lrB,
                     u_sb=u_sb: e.matmul(
                    bank.ap()[:, :], lrB[0:16, dt * 128:(dt + 1) * 128],
                    u_sb[0:16, tch * 512:(tch + 1) * 512],
                    start=False, stop=True),
                    inc="pe" if tch == 1 else None)
            pm = b.v("pe")
            evacs = [("dve", evac[dt - 2][2])] if dt >= 2 else []
            dm = head_norm_rope(dt, (A[2 * buf], A[2 * buf + 1]), out3,
                                gain(dt) if gain else None, evacs)
            evac[dt] = (pm, b.v("act"), dm)
        return b.v("dve")

    qm = proj_heads(wq, qB, uq, q3, NTILE,
                    lambda dt: qgain_ap[0:1, dt:dt + 1], "q1")
    km = proj_heads(wk, kB, uk, k3, 4, None, "k1")

    entry_pe = b.v("pe")
    entry_act = b.v("act")
    b.op("pool", lambda e: e.dma_start(
        out=vwbuf.rearrange("p (c n) -> p c n", c=NTILE),
        in_=chunked_cols(wv, 0, 512)),
        waits=(("pe", entry_pe),), inc="v1_wdma", n=16)
    wv_ = b.v("v1_wdma")
    evacm = {}
    for tt in range(8):
        bank = A[4 + tt % 2]
        pe_w = []
        if tt == 0:
            pe_w.append(("v1_wdma", wv_))
            pe_w.append(("act", entry_act))
        if tt >= 2:
            pe_w.append(("act", evacm[tt - 2]))
        for kc in range(NTILE):
            b.op("pe", lambda e, tt=tt, kc=kc, bank=bank: e.matmul(
                bank.ap()[:, :], xn3[:, kc, tt * 128:(tt + 1) * 128],
                vwbuf.rearrange("p (c n) -> p c n", c=NTILE)[:, kc, :],
                start=(kc == 0), stop=False),
                waits=pe_w if kc == 0 else ())
        b.op("pe", lambda e, tt=tt, bank=bank: e.matmul(
            bank.ap()[:, :], uv[0:16, tt * 128:(tt + 1) * 128],
            p.vB_sb.ap()[0:16, :], start=False, stop=True), inc="pe")
        pm = b.v("pe")
        b.op("act", lambda e, tt=tt, bank=bank: e.copy(
            out=vt3[:, tt, :], in_=bank.ap()[:, :]),
            waits=(("pe", pm),), inc="act")
        evacm[tt] = b.v("act")
    vm_tm = b.v("act")
    evac2 = {}
    for dt in range(4):
        bank0, bank1 = A[6], A[7]
        pe_w = []
        if dt >= 1:
            pe_w.append(("act", evac2[dt - 1]))
        for tch in range(2):
            bank = (bank0, bank1)[tch]
            for kc in range(NTILE):
                b.op("pe", lambda e, dt=dt, tch=tch, kc=kc, bank=bank: e.matmul(
                    bank.ap()[:, :],
                    vwbuf.rearrange("p (c n) -> p c n", c=NTILE)[:, kc,
                                                     dt * 128:(dt + 1) * 128],
                    xn3[:, kc, tch * 512:(tch + 1) * 512],
                    start=(kc == 0), stop=False),
                    waits=pe_w if (tch == 0 and kc == 0) else ())
            b.op("pe", lambda e, dt=dt, tch=tch, bank=bank: e.matmul(
                bank.ap()[:, :], p.vB_sb.ap()[0:16, dt * 128:(dt + 1) * 128],
                uv[0:16, tch * 512:(tch + 1) * 512],
                start=False, stop=True), inc="pe" if tch == 1 else None)
        pm = b.v("pe")
        for tch in range(2):
            bank = (bank0, bank1)[tch]
            b.op("act", lambda e, dt=dt, tch=tch, bank=bank: e.copy(
                out=vf3[:, dt, tch * 512:(tch + 1) * 512], in_=bank.ap()[:, :]),
                waits=((("pe", pm),) if tch == 0 else ()),
                inc="act" if tch == 1 else None)
        evac2[dt] = b.v("act")
        am = evac2[dt]
        b.op("act", lambda e, dt=dt: e.activation(
            out=p.l1sq.rearrange("p (c t) -> p c t", c=2)[:, 0, :],
            in_=vf3[:, dt, :], func=AF.Square), inc="act")
        am2 = b.v("act")
        dw = b.v("dve")
        for tch in range(2):
            b.op("pe", lambda e, tch=tch: e.matmul(
                A[4 + tch].ap()[0:1, :], p.c_ones_col.ap()[:, 0:1],
                p.l1sq.rearrange("p (c t) -> p c t", c=2)[:, 0,
                                                          tch * 512:(tch + 1) * 512],
                start=True, stop=True),
                waits=((("act", am2), ("dve", dw)) if tch == 0 else ()),
                inc="pe" if tch == 1 else None)
        pm2 = b.v("pe")
        b.op("act", lambda e: e.activation(out=sm[0:1, 0:512],
                                           in_=A[4].ap()[0:1, :], func=AF.Sqrt),
             waits=(("pe", pm2),))
        b.op("act", lambda e: e.activation(out=sm[0:1, 512:1024],
                                           in_=A[5].ap()[0:1, :], func=AF.Sqrt),
             inc="act")
        am3 = b.v("act")
        b.op("dve", lambda e: e.tensor_scalar_max(sm[0:1, 0:1024],
                                                  sm[0:1, 0:1024], 1e-12),
             waits=(("act", am3),), inc="dve")
        dm = b.v("dve")
        b.op("dve", lambda e: e.reciprocal(sm[0:1, 0:1024], sm[0:1, 0:1024]),
             waits=(("dve", dm),), inc="dve")
        dm = b.v("dve")
        b.op("dve", lambda e: e.tensor_copy(p.ln_small.ap()[0:1, 0:1024],
                                            sm[0:1, 0:1024]),
             waits=(("dve", dm),), inc="dve")
        dm = b.v("dve")
        for tch in range(2):
            b.op("pe", lambda e, tch=tch: e.matmul(
                A[4 + tch].ap()[:, :], p.c_ones_row.ap()[0:1, :],
                p.ln_small.ap()[0:1, tch * 512:(tch + 1) * 512],
                start=True, stop=True),
                waits=((("dve", dm),) if tch == 0 else ()),
                inc="pe" if tch == 1 else None)
        pm3 = b.v("pe")
        for tch in range(2):
            b.op("dve", lambda e, dt=dt, tch=tch: e.tensor_mul(
                vh3[:, dt, tch * 512:(tch + 1) * 512],
                vf3[:, dt, tch * 512:(tch + 1) * 512],
                A[4 + tch].ap()[:, :]),
                waits=((("pe", pm3),) if tch == 0 else ()),
                inc="dve" if tch == 1 else None)
    vhm = b.v("dve")
    return qm, km, vm_tm, vhm


def build_l1_qkv(debug=False, p=None, io=None, permute_q=False):
    standalone = p is None
    if standalone:
        p = Prog("l1qkv")
    io = io or {}
    def din(name, shape, dtype):
        return io[name] if name in io else p.din(name, shape, dtype)
    def dout(name, shape, dtype):
        return io[name] if name in io else p.dout(name, shape, dtype)
    b = p.b
    x_in = din("x", [D, T], F32)
    wq = din("wq", [D, D], F32)
    wk = din("wk", [D, 512], F32)
    wv = din("wv", [D, 512], F32)
    qA = din("qA", [D, 16], F32)
    qB = din("qB", [16, D], F32)
    kA = din("kA", [D, 16], F32)
    kB = din("kB", [16, 512], F32)
    vA = din("vA", [D, 16], F32)
    vB = din("vB", [16, 512], F32)
    rc = din("rope1c", [64, T], BF16)
    rs = din("rope1s", [64, T], BF16)
    qg = din("qgain", [1, 16], F32)
    q_out = dout("q1", [D, T], BF16)
    k_out = dout("k1", [512, T], BF16)
    v_out = dout("v1", [T, 512], BF16)
    vh_out = dout("vhat", [512, T], BF16)
    xn_out = dout("xn3_dbg", [D, T], BF16) if debug else None

    xt = p.sb("xt", [128, NTILE * T], F32)
    xn = p.sb("xn", [128, NTILE * T], BF16)
    qs = p.sb("qs", [128, NTILE * T], BF16)
    ks = p.sb("ks", [128, 4 * T], BF16)
    vstm = p.sb("vstm", [128, 8 * 512], BF16)
    vsfm = p.sb("vsfm", [128, 4 * T], BF16)
    vhat = p.sb("vhat_sb", [128, 4 * T], BF16)
    wbuf = p.sb("wbuf", [128, 4096], BF16)
    rbuf = p.sb("rbuf", [128, 2 * T], BF16)
    ubuf = p.sb("ubuf", [16, 3 * T], BF16)
    p.lrbuf = p.sb("lrbuf", [128, 256], BF16)
    p.vB_sb = p.sb("vBsb", [16, 512], BF16)
    qB_sb = p.sb("qBsb", [16, D], BF16)
    kB_sb = p.sb("kBsb", [16, 512], BF16)
    
    p.l1pre = p.sb("l1pre", [128, 2 * T], BF16).ap()
    l1small = p.sb("l1small", [1, 1024], F32)
    p.l1small = l1small.ap()
    p.l1sq = rbuf.ap()
    if standalone:
        p.make_common()
    p.c_rope1c = p.sb("rope1c_t", [64, T], BF16)
    p.c_rope1s = p.sb("rope1s_t", [64, T], BF16)
    qg_sb = p.sb("qg_sb", [1, 16], F32)
    stats = qs.ap().bitcast(F32)[0:1, 0:3072]

    b.op("sync", lambda e: e.dma_start(
        out=sb3(xt, NTILE, T), in_=x_in.rearrange("(c p) t -> p c t", p=128)),
        inc="adma", n=16)
    b.op("sync", lambda e: e.dma_start(out=p.c_rope1c.ap()[:, :], in_=rc),
         inc="adma", n=16)
    b.op("sync", lambda e: e.dma_start(out=p.c_rope1s.ap()[:, :], in_=rs),
         inc="adma", n=16)
    b.op("sync", lambda e: e.dma_start(out=qg_sb.ap()[:, :], in_=qg),
         inc="adma", n=16)
    b.op("pool", lambda e: e.dma_start(out=p.vB_sb.ap()[:, :], in_=vB),
         inc="adma", n=16)
    b.op("pool", lambda e: e.dma_start(out=qB_sb.ap()[:, :], in_=qB),
         inc="adma", n=16)
    b.op("pool", lambda e: e.dma_start(out=kB_sb.ap()[:, :], in_=kB),
         inc="adma", n=16)
    ad = b.v("adma")
    xn_ready = emit_layernorm(p, sb3(xt, NTILE, T), sb3(xn, NTILE, T),
                              stats, rbuf.ap(), entry=(("adma", ad),))
    u3 = ubuf.ap().rearrange("p (c t) -> p c t", c=3)
    vwbuf = xt.ap().bitcast(BF16)[:, 0:8192]
    qm, km, vm, vhm = emit_l1_qkv(
        p, xn.ap(), wq, wk, wv, qA, qB_sb.ap(), kA, kB_sb.ap(), vA,
        p.vB_sb.ap(), qg_sb.ap(), qs.ap(), ks.ap(), vstm.ap(), vsfm.ap(),
        vhat.ap(), u3[:, 0, :], u3[:, 1, :], u3[:, 2, :], wbuf, vwbuf,
        ("dve", xn_ready))
    b.op("sync", lambda e: e.dma_start(
        out=q_out.rearrange("(c p) t -> p c t", p=128), in_=sb3(qs, NTILE, T)),
        waits=(("dve", qm),), inc="osem", n=16)
    b.op("sync", lambda e: e.dma_start(
        out=k_out.rearrange("(c p) t -> p c t", p=128), in_=sb3(ks, 4, T)),
        waits=(("dve", km),), inc="osem", n=16)
    b.op("sync", lambda e: e.dma_start(
        out=v_out.rearrange("(c p) t -> p c t", p=128), in_=sb3(vstm, 8, 512)),
        waits=(("act", vm),), inc="osem", n=16)
    b.op("sync", lambda e: e.dma_start(
        out=vh_out.rearrange("(c p) t -> p c t", p=128), in_=sb3(vhat, 4, T)),
        waits=(("dve", vhm),), inc="osem", n=16)
    n = 4
    if debug:
        b.op("sync", lambda e: e.dma_start(
            out=xn_out.rearrange("(c p) t -> p c t", p=128),
            in_=sb3(xn, NTILE, T)),
            waits=(("dve", xn_ready),), inc="osem", n=16)
        u_out = dout("u_dbg", [16, 3 * T], BF16)
        b.op("sync", lambda e: e.dma_start(out=u_out, in_=ubuf.ap()[:, :]),
             waits=(("dve", qm),), inc="osem", n=16)
        n = 6
    b.op("sync", lambda e: None, waits=(("osem", n * 16),))
    return p


L1SCALE = 128.0 ** -0.5


def emit_l1_negb(p, qg_dram, negB1, scr_row):
    b = p.b
    A = p.banks
    b.op("sync", lambda e: e.dma_start(out=scr_row[0:1, 0:16], in_=qg_dram),
         inc="adma", n=16)
    ad = b.v("adma")
    b.op("act", lambda e: e.activation(out=scr_row[0:1, 16:32],
                                       in_=scr_row[0:1, 0:16], func=AF.Abs),
         waits=(("adma", ad),), inc="act")
    am = b.v("act")
    b.op("dve", lambda e: e.tensor_scalar(
        scr_row[0:1, 16:32], scr_row[0:1, 16:32], -(128.0 * L1SCALE), 44.0,
        ALU.mult, ALU.add), waits=(("act", am),), inc="dve")
    dm = b.v("dve")
    b.op("dve", lambda e: e.tensor_copy(p.ln_small.ap()[0:1, 32:48],
                                        scr_row[0:1, 16:32]),
         waits=(("dve", dm),), inc="dve")
    dm = b.v("dve")
    b.op("pe", lambda e: e.matmul(A[7].ap()[:, 0:16], p.c_ones_row.ap()[0:1, :],
                                  p.ln_small.ap()[0:1, 32:48],
                                  start=True, stop=True),
         waits=(("dve", dm),), inc="pe")
    pm = b.v("pe")
    b.op("act", lambda e: e.copy(out=negB1.ap()[:, :], in_=A[7].ap()[:, 0:16]),
         waits=(("pe", pm),), inc="act")
    return b.v("act")


def emit_l1_attention(p, qs, ks, vs, vhat, attn_c, masks, ebuf, negB1,
                      b12s, osbuf, smalls, ready):
    b = p.b
    A = p.banks
    q3 = qs.rearrange("p (c t) -> p c t", c=NTILE)
    k3 = ks.rearrange("p (c t) -> p c t", c=4)
    v3 = vs.rearrange("p (c t) -> p c t", c=16)
    vh3 = vhat.rearrange("p (c t) -> p c t", c=4)
    m3 = masks.rearrange("p (c t) -> p c t", c=16)
    e3 = ebuf.rearrange("p (c t) -> p c t", c=2)
    ac3 = attn_c.rearrange("p (c t) -> p c t", c=NTILE)
    sm = smalls
    first = True
    prev_evac_dve = 0
    for h in range(16):
        kh = h // 4
        for slot in (0, 1):
            kts = SLOT_KTS[slot]
            n = len(kts)
            exp_m = {}
            pvm = {}
            maskd = {}
            for i, kt in enumerate(kts):
                par = i % 2
                bS = A[4 + par]
                w = []
                if first:
                    w += ready
                    first = False
                if i >= 2:
                    w.append(("act", exp_m[i - 2]))
                if i == 0 and prev_evac_dve:
                    w.append(("dve", prev_evac_dve))
                b.op("pe", lambda e, h=h, slot=slot, kt=kt, kh=kh, bS=bS: e.matmul(
                    bS.ap()[:, :],
                    k3[:, kh, kt * 128:(kt + 1) * 128],
                    q3[:, h, slot * 512:(slot + 1) * 512],
                    start=True, stop=True), waits=w, inc="pe")
                pm = b.v("pe")
                wa = [("pe", pm)]
                if i >= 2:
                    wa.append(("pe", pvm[i - 2]))
                b.op("act", lambda e, h=h, par=par, bS=bS: e.activation(
                    out=e3[:, par, :], in_=bS.ap()[:, :], func=AF.Exp,
                    bias=negB1.ap()[:, h:h + 1], scale=L1SCALE),
                    waits=wa, inc="act")
                exp_m[i] = b.v("act")
                mi = MASKED.get((slot, kt))
                if mi is not None:
                    b.op("dve", lambda e, par=par, mi=mi: e.tensor_mul(
                        e3[:, par, :], e3[:, par, :], m3[:, mi, :]),
                        waits=(("act", exp_m[i]),), inc="dve")
                    maskd[i] = ("dve", b.v("dve"))
                else:
                    maskd[i] = ("act", exp_m[i])
                b.op("pe", lambda e, par=par, i=i, n=n: e.matmul(
                    A[2].ap()[0:1, :], p.c_ones_col.ap()[:, 0:1], e3[:, par, :],
                    start=(i == 0), stop=(i == n - 1)), waits=(maskd[i],))
                b.op("pe", lambda e, kt=kt, kh=kh, par=par, i=i, n=n: e.matmul(
                    A[0].ap()[:, :], v3[:, kt, kh * 128:(kh + 1) * 128],
                    e3[:, par, :], start=(i == 0), stop=(i == n - 1)),
                    inc="pe")
                pvm[i] = b.v("pe")
            last_pe = pvm[n - 1]
            b.op("dve", lambda e: e.reciprocal(sm[0:1, 0:512], A[2].ap()[0:1, :]),
                 waits=(("pe", last_pe),), inc="dve")
            dm = b.v("dve")
            b.op("dve", lambda e: e.tensor_scalar_min(
                p.ln_small.ap()[0:1, 1024:1536], sm[0:1, 0:512], 1e30),
                 waits=(("dve", dm),), inc="dve")
            dm = b.v("dve")
            b.op("dve", lambda e, h=h, slot=slot, kh=kh: e.tensor_mul(
                b12s[:, 0:512], A[0].ap()[:, :],
                vh3[:, kh, slot * 512:(slot + 1) * 512]), inc="dve")
            dm2 = b.v("dve")
            b.op("pe", lambda e: e.matmul(
                A[3].ap()[0:1, :], p.c_ones_col.ap()[:, 0:1], b12s[:, 0:512],
                start=True, stop=True), waits=(("dve", dm2),), inc="pe")
            pm = b.v("pe")
            b.op("dve", lambda e: e.tensor_copy(p.ln_small.ap()[0:1, 1536:2048],
                                                A[3].ap()[0:1, :]),
                 waits=(("pe", pm),), inc="dve")
            dm3 = b.v("dve")
            b.op("pe", lambda e: e.matmul(A[6].ap()[:, :], p.c_ones_row.ap()[0:1, :],
                                          p.ln_small.ap()[0:1, 1536:2048],
                                          start=True, stop=True),
                 waits=(("dve", dm3),))
            b.op("pe", lambda e: e.matmul(A[7].ap()[:, :], p.c_ones_row.ap()[0:1, :],
                                          p.ln_small.ap()[0:1, 1024:1536],
                                          start=True, stop=True), inc="pe")
            pm = b.v("pe")
            b.op("dve", lambda e, h=h, slot=slot, kh=kh: e.tensor_mul(
                osbuf[:, 0:512], vh3[:, kh, slot * 512:(slot + 1) * 512],
                A[6].ap()[:, :]), waits=(("pe", pm),), inc="dve")
            dm4 = b.v("dve")
            b.op("dve", lambda e: e.tensor_sub(
                osbuf[:, 512:1024], A[0].ap()[:, :], osbuf[:, 0:512]),
                waits=(("dve", dm4),), inc="dve")
            dm5 = b.v("dve")
            b.op("dve", lambda e, h=h, slot=slot: e.scalar_tensor_tensor(
                ac3[:, h, slot * 512:(slot + 1) * 512],
                osbuf[:, 512:1024], 1.0, A[7].ap()[:, :],
                ALU.mult, ALU.mult), waits=(("dve", dm5),), inc="dve")
            prev_evac_dve = b.v("dve")
    return b.v("dve")


def build_l1_rest(debug=False, p=None, io=None, permute_q=False):
    standalone = p is None
    if standalone:
        p = Prog("l1rest")
    io = io or {}
    def din(name, shape, dtype):
        return io[name] if name in io else p.din(name, shape, dtype)
    def dout(name, shape, dtype):
        return io[name] if name in io else p.dout(name, shape, dtype)
    b = p.b
    x_in = din("x", [D, T], F32)
    q_in = din("q1", [D, T], BF16)
    k_in = din("k1full", [512, S], BF16)
    v_in = din("v1full", [S, 512], BF16)
    vh_in = din("vhat", [512, T], BF16)
    m_in = din("masks", [2048, 512], BF16)
    qg = din("qgain", [1, 16], F32)
    wproj = din("wproj", [D, D], F32)
    pA = din("pA", [D, 16], F32)
    pB = din("pB", [16, D], F32)
    wupT = din("wupT", [D, 8192], F32)
    wgateT = din("wgateT", [D, 8192], F32)
    wdownT = din("wdownT", [8192, D], F32)
    su_ug = din("su_ug", [128, 16], F32)
    su_dp = din("su_dp", [128, 64], F32)
    ws_up = din("ws_up", [128, 64], F32)
    ws_gate = din("ws_gate", [128, 64], F32)
    ws_dp = din("ws_dp", [128, 16], F32)
    h128_in = din("h128_in", [128, 128], BF16)
    out_x = dout("out", [D, T], F32)
    x3_out = dout("x3", [D, T], F32)
    ac_out = dout("attn_c1", [D, T], BF16) if debug else None
    xf_out = dout("xf_dbg", [D, T], BF16) if debug else None

    arena = p.sb("arena", [128, 90112], BF16)
    wbuf = p.sb("wbuf", [128, 8192], BF16)
    if standalone:
        p.make_common()
    p.c_h128 = p.sb("h128", [128, 128], BF16)
    negB1 = p.sb("negB1", [128, 16], F32)
    p.lrbuf = p.sb("lrbuf", [128, 256], BF16)
    sut = p.sb("sut", [128, 16], F32)
    sdpt = p.sb("sdpt", [128, 64], F32)
    wsu_t = p.sb("wsut", [128, 64], F32)
    wsg_t = p.sb("wsgt", [128, 64], F32)
    wdp_t = p.sb("wdpt", [128, 16], F32)
    ubuf = p.sb("ubuf", [16, T], BF16)

    ar = arena.ap()
    arf = ar.bitcast(F32)
    qs = ar[:, 0:16384]
    ks = ar[:, 16384:24576]
    vs = ar[:, 24576:32768]
    vhat = ar[:, 32768:36864]
    masks = ar[:, 36864:45056]
    ebuf = ar[:, 45056:46080]
    b12s = ar[:, 46080:47104]
    osbuf = arf[:, 23552:24576]
    smalls_ap = arf[0:1, 24576:25088]
    attn_c = ar[:, 53248:69632]
    x3_arena = arf[:, 0:16384].rearrange("p (c t) -> p c t", c=NTILE)
    xn4 = ar[:, 69632:86016]
    rbuf4 = ar[:, 50176:52224]
    stats4 = arf[0:1, 17920:20992]
    hidden = ar[:, 0:65536]
    fw2_sc = arf[:, 43008:45056]
    fwht_sc = arf[:, 34816:43008]
    x2ring = arf[:, 38912:40960]
    xold = arf[:, 43008:45056]
    w2buf = ar[:, 69632:77824]
    _loads = [
        ("q1", lambda e: e.dma_start(
            out=sb3_like(qs, NTILE),
            in_=q_in.rearrange("(c p) t -> p c t", p=128))),
        ("k1", lambda e: e.dma_start(
            out=sb3_like(ks, 4), in_=k_in.rearrange("(c p) t -> p c t", p=128))),
        ("v1", lambda e: e.dma_start(
            out=sb3_like(vs, 16), in_=v_in.rearrange("(c p) t -> p c t", p=128))),
        ("vh", lambda e: e.dma_start(
            out=sb3_like(vhat, 4),
            in_=vh_in.rearrange("(c p) t -> p c t", p=128))),
        ("mk", lambda e: e.dma_start(
            out=masks.rearrange("p (c n) -> p c n", c=16),
            in_=m_in.rearrange("(c p) t -> p c t", p=128))),
        ("h1", lambda e: e.dma_start(out=p.c_h128.ap()[:, :], in_=h128_in)),
        ("su", lambda e: e.dma_start(out=sut.ap()[:, :], in_=su_ug)),
        ("sd", lambda e: e.dma_start(out=sdpt.ap()[:, :], in_=su_dp)),
        ("wu", lambda e: e.dma_start(out=wsu_t.ap()[:, :], in_=ws_up)),
        ("wg", lambda e: e.dma_start(out=wsg_t.ap()[:, :], in_=ws_gate)),
        ("wd", lambda e: e.dma_start(out=wdp_t.ap()[:, :], in_=ws_dp)),
    ]
    for _nm, _fn in _loads:
        b.op("sync", _fn, inc="adma", n=16)
    ad = b.v("adma")
    b.op("dve", lambda e: e.tensor_scalar_mul(
        wsu_t.ap()[:, :], wsu_t.ap()[:, :], 1.0 / math.sqrt(2048.0)),
        waits=(("adma", ad),), inc="dve")
    dm = b.v("dve")
    b.op("dve", lambda e: e.tensor_mul(wsu_t.ap()[:, :], wsu_t.ap()[:, :],
                                       sdpt.ap()[:, :]),
         waits=(("dve", dm),), inc="dve")
    dm = b.v("dve")
    b.op("dve", lambda e: e.tensor_scalar_mul(
        wsg_t.ap()[:, :], wsg_t.ap()[:, :], 1.0 / math.sqrt(2048.0)),
        waits=(("dve", dm),), inc="dve")
    dm = b.v("dve")
    b.op("dve", lambda e: e.tensor_scalar_mul(
        wdp_t.ap()[:, :], wdp_t.ap()[:, :], SQ8192),
        waits=(("dve", dm),), inc="dve")
    prep_m = b.v("dve")

    nb_m = emit_l1_negb(p, qg, negB1, smalls_ap)
    ready = [("adma", ad), ("act", nb_m)]
    attn_m = emit_l1_attention(p, qs, ks, vs, vhat, attn_c, masks, ebuf,
                               negB1, b12s, osbuf, smalls_ap, ready)
    if debug:
        b.op("sync", lambda e: e.dma_start(
            out=ac_out.rearrange("(c p) t -> p c t", p=128),
            in_=sb3_like(attn_c, NTILE)),
            waits=(("dve", attn_m),), inc="spill", n=16)
    pB_sb_ap = ar[0:16, 47104:49152]
    b.op("pool", lambda e: e.dma_start(out=pB_sb_ap, in_=pB),
         waits=(("dve", attn_m),), inc="adma", n=16)
    ad2 = b.v("adma")
    b.op("pe", lambda e: None, waits=(("adma", ad2),))
    up_m = emit_lowrank_u(p, pA, attn_c, ubuf.ap()[:, :], ("dve", attn_m),
                          "up", p.banks[6])
    op_m, spill_cnt = emit_outproj_residual(
        p, wproj, attn_c, x_in, x3_arena, x3_out, wbuf, xold, ("dve", attn_m),
        lowrank=None, lr=(pB_sb_ap, ubuf.ap(), up_m))
    ln4_m = emit_layernorm(p, x3_arena,
                           xn4.rearrange("p (c t) -> p c t", c=NTILE),
                           stats4, rbuf4,
                           entry=(("dve", op_m),), chunk_scale=sut.ap())
    xf_m = emit_fwht(p, xn4.rearrange("p (c t) -> p c t", c=NTILE),
                     fw2_sc, 16, ln4_m)
    if debug:
        b.op("sync", lambda e: e.dma_start(
            out=xf_out.rearrange("(c p) t -> p c t", p=128),
            in_=xn4.rearrange("p (c t) -> p c t", c=NTILE)),
            waits=(("dve", xf_m),), inc="spill", n=16)
        b.op("dve", lambda e: e.engine_nop(), waits=(("spill", b.v("spill")),))
        b.op("dve", lambda e: e.engine_nop(), inc="dve")
        xf_m = b.v("dve")
    spill_all = b.v("spill")
    dbg = None
    if debug:
        dbg = {"hid_pre": p.dout("hid_pre", [8192, T], BF16),
               "hid_post": p.dout("hid_post", [8192, T], BF16)}
    o_cnt = emit_ffn0(p, wgateT, wupT, wdownT, xn4, hidden, wbuf, w2buf,
                      x3_out, out_x, xold, x2ring, fwht_sc,
                      ("dve", xf_m), spill_all,
                      wscale_up=wsu_t, wscale_gate=wsg_t, sudp=None,
                      wscale_dp=wdp_t, silu_buf=ar[:, 86016:87040], dbg=dbg)
    if debug:
        ws_out = dout("wsu_dbg", [128, 144], F32)
        b.op("sync", lambda e: e.dma_start(out=ws_out[:, 0:64], in_=wsu_t.ap()[:, :]),
             waits=(("dve", prep_m),), inc="spill", n=16)
        b.op("sync", lambda e: e.dma_start(out=ws_out[:, 64:128], in_=wsg_t.ap()[:, :]),
             inc="spill", n=16)
        b.op("sync", lambda e: e.dma_start(out=ws_out[:, 128:144], in_=wdp_t.ap()[:, :]),
             inc="spill", n=16)
    b.op("sync", lambda e: None,
         waits=(("osem", o_cnt), ("spill", b.v("spill"))))
    return p



def _hadamard128():
    H = np.array([[1.0]])
    while H.shape[0] < 128:
        H = np.block([[H, H], [H, -H]])
    return H.astype(BF)


_NC_CACHE = {}


def _get_nc(name, builder):
    if name not in _NC_CACHE:
        p = builder(debug=False)
        _NC_CACHE[name] = p.finish()
    return _NC_CACHE[name]


def kernel(**inputs):
    return kernel_fused2(**inputs)


def kernel_4launch(**inputs):
    inp = {k: np.asarray(v) for k, v in inputs.items()}
    f32 = np.float32
    cont = np.ascontiguousarray

    x_full = inp["L0_attn_hidden_states"].astype(f32)
    wqT = cont(inp["L0_attn_Wq"].T.astype(f32))
    wkT = cont(inp["L0_attn_Wk"].T.astype(f32))
    wvT = cont(inp["L0_attn_Wv"].T.astype(f32))
    woT = cont(inp["L0_attn_Wo"].T.astype(f32))
    w1T = cont(inp["L0_ffn_W1"].T.astype(f32))
    w3T = cont(inp["L0_ffn_W3"].T.astype(f32))
    w2T = cont(inp["L0_ffn_W2"].T.astype(f32))
    lam = np.concatenate([inp["L0_attn_lambda_q1"], inp["L0_attn_lambda_k1"],
                          inp["L0_attn_lambda_q2"], inp["L0_attn_lambda_k2"]]
                         )[None, :].astype(f32)
    sublnw = inp["L0_attn_subln_weight"].reshape(128, 1).astype(f32)
    wq1 = cont(inp["L1_attn_Wq"].astype(f32))
    wk1 = cont(inp["L1_attn_Wk"].astype(f32))
    wv1 = cont(inp["L1_attn_Wv"].astype(f32))
    qA = cont(inp["L1_attn_q_A"].astype(f32)); qB = cont(inp["L1_attn_q_B"].astype(f32))
    kA = cont(inp["L1_attn_k_A"].astype(f32)); kB = cont(inp["L1_attn_k_B"].astype(f32))
    vA = cont(inp["L1_attn_v_A"].astype(f32)); vB = cont(inp["L1_attn_v_B"].astype(f32))
    qgain = inp["L1_attn_q_gain"][None, :].astype(f32)
    wproj = cont(inp["L1_attn_Wproj"].astype(f32))
    pA = cont(inp["L1_attn_p_A"].astype(f32)); pB = cont(inp["L1_attn_p_B"].astype(f32))
    wupT = cont(inp["L1_ffn_W_up"].T.astype(f32))
    wgateT = cont(inp["L1_ffn_W_gate"].T.astype(f32))
    wdownT = cont(inp["L1_ffn_W_down"].T.astype(f32))
    su_ug = cont(inp["L1_ffn_SU_ug"].reshape(16, 128).T.astype(f32))
    su_dp = cont(inp["L1_ffn_SU_dp"].reshape(64, 128).T.astype(f32))
    ws_up = cont(inp["L1_ffn_Wscale_ug"][:8192].reshape(64, 128).T.astype(f32))
    ws_gate = cont(inp["L1_ffn_Wscale_ug"][8192:].reshape(64, 128).T.astype(f32))
    ws_dp = cont(inp["L1_ffn_Wscale_dp"].reshape(16, 128).T.astype(f32))
    h128 = _hadamard128()
    perm = q_perm_rows()

    cores = list(range(8))
    pos = {h: own_pos(h) for h in (0, 1)}
    r0 = {h: rope0_tables(h) for h in (0, 1)}
    r1 = {h: rope1_tables(h) for h in (0, 1)}
    mk = {h: np.ascontiguousarray(mask_tables(h).reshape(2048, 512))
          for h in (0, 1)}

    x_fm = []
    for c in cores:
        bb, h = divmod(c, 2)
        x_fm.append(cont(x_full[bb][pos[h]].T))

    ncA = _get_nc("A", build_l0_qkv)
    mapsA = []
    for c in cores:
        bb, h = divmod(c, 2)
        mapsA.append({"x": x_fm[c], "wqT": wqT, "wkT": wkT, "wvT": wvT,
                      "rope0c": r0[h][0], "rope0s": r0[h][1]})
    resA = run_bass_kernel_spmd(ncA, mapsA, core_ids=cores)

    kfull = {}
    vfull = {}
    for bb in range(4):
        kvo_e = resA.results[2 * bb]["kv"]
        kvo_o = resA.results[2 * bb + 1]["kv"]
        kg = np.empty((1024, 2048), dtype=kvo_e.dtype)
        kg[:, pos[0]] = kvo_e[0:1024]
        kg[:, pos[1]] = kvo_o[0:1024]
        vg = np.empty((2048, 1024), dtype=kvo_e.dtype)
        vg[pos[0], :] = kvo_e[1024:2048]
        vg[pos[1], :] = kvo_o[1024:2048]
        kfull[bb] = kg
        vfull[bb] = vg

    ncB = _get_nc("B", build_l0_rest)
    mapsB = []
    for c in cores:
        bb, h = divmod(c, 2)
        q = resA.results[c]["q"]
        qp = np.empty_like(q)
        for srow in range(32):
            qp[srow * 64:(srow + 1) * 64] = \
                q[perm[srow] * 64:(perm[srow] + 1) * 64]
        mapsB.append({"x": x_fm[c], "q": qp, "kfull": kfull[bb],
                      "vfull": vfull[bb], "masks": mk[h], "lam": lam,
                      "sublnw": sublnw, "woT": woT, "w1T": w1T, "w3T": w3T,
                      "w2T": w2T, "h128_in": h128})
    resB = run_bass_kernel_spmd(ncB, mapsB, core_ids=cores)

    ncC = _get_nc("C", build_l1_qkv)
    mapsC = []
    for c in cores:
        bb, h = divmod(c, 2)
        mapsC.append({"x": resB.results[c]["x2"], "wq": wq1, "wk": wk1,
                      "wv": wv1, "qA": qA, "qB": qB, "kA": kA, "kB": kB,
                      "vA": vA, "vB": vB, "rope1c": r1[h][0],
                      "rope1s": r1[h][1], "qgain": qgain})
    resC = run_bass_kernel_spmd(ncC, mapsC, core_ids=cores)

    k1full = {}
    v1full = {}
    for bb in range(4):
        ke = resC.results[2 * bb]["k1"]; ko = resC.results[2 * bb + 1]["k1"]
        kg = np.empty((512, 2048), dtype=ke.dtype)
        kg[:, pos[0]] = ke
        kg[:, pos[1]] = ko
        ve = resC.results[2 * bb]["v1"]; vo = resC.results[2 * bb + 1]["v1"]
        vg = np.empty((2048, 512), dtype=ve.dtype)
        vg[pos[0], :] = ve
        vg[pos[1], :] = vo
        k1full[bb] = kg
        v1full[bb] = vg

    ncD = _get_nc("D", build_l1_rest)
    mapsD = []
    for c in cores:
        bb, h = divmod(c, 2)
        mapsD.append({"x": resB.results[c]["x2"], "q1": resC.results[c]["q1"],
                      "k1full": k1full[bb], "v1full": v1full[bb],
                      "vhat": resC.results[c]["vhat"], "masks": mk[h],
                      "qgain": qgain, "wproj": wproj, "pA": pA, "pB": pB,
                      "wupT": wupT, "wgateT": wgateT, "wdownT": wdownT,
                      "su_ug": su_ug, "su_dp": su_dp, "ws_up": ws_up,
                      "ws_gate": ws_gate, "ws_dp": ws_dp, "h128_in": h128})
    resD = run_bass_kernel_spmd(ncD, mapsD, core_ids=cores)

    out = np.empty((4, 2048, 2048), dtype=np.float32)
    for c in cores:
        bb, h = divmod(c, 2)
        out[bb][pos[h]] = resD.results[c]["out"].T
    return out



RG_PAIRS = [[0, 1], [2, 3], [4, 5], [6, 7]]


def build_fused():
    p = Prog("mk", phased=True)
    b = p.b
    nc = p.nc
    p.make_common(rope=True)

    x0 = p.din("x", [D, T], F32)
    h128_in = p.din("h128_in", [128, 128], BF16)
    masks_in = p.din("masks", [2048, 512], BF16)
    qg_in = p.din("qgain", [1, 16], F32)
    dr = lambda n, s, dt: nc.dram_tensor(n, s, dt, kind="Internal").ap()
    q_h = dr("qh", [D, T], BF16)
    kv_own = dr("kv_own", [2048, T], BF16)
    kv_all = dr("kv_all", [4096, T], BF16)
    kfullB = dr("kfullB", [1024, S], BF16)
    vfullB = dr("vfullB", [S, 1024], BF16)
    x1_h = dr("x1h", [D, T], F32)
    x2_h = dr("x2h", [D, T], F32)
    q1_h = dr("q1h", [D, T], BF16)
    kv1_own = dr("kv1_own", [1024, 1024], BF16)
    kv1_all = dr("kv1_all", [2048, 1024], BF16)
    k1fullD = dr("k1fullD", [512, S], BF16)
    v1fullD = dr("v1fullD", [S, 512], BF16)
    vhat_h = dr("vhath", [512, T], BF16)
    x3_h = dr("x3h", [D, T], F32)
    out_x = p.dout("out", [D, T], F32)

    p.phase = "A_"
    ioA = {"x": x0, "q": q_h, "kv": kv_own}
    build_l0_qkv(debug=False, p=p, io=ioA, permute_q=True)

    p.new_phase("G0_")
    b.op("pool", lambda e: e.collective_compute(
        "AllGather", ALU.bypass, replica_groups=RG_PAIRS,
        ins=[kv_own[:, :]], outs=[kv_all[:, :]]), inc="coll", n=16)
    cv = b.v("coll")
    KSRC = [(0, 512, 0, 512), (512, 1024, 1, 512), (1024, 1536, 1, 0),
            (1536, 2048, 0, 0)]
    for gi, (g0, g1, blk, c0) in enumerate(KSRC):
        b.op("pool", lambda e, g0=g0, g1=g1, blk=blk, c0=c0: e.dma_start(
            out=kfullB[:, g0:g1],
            in_=kv_all[blk * 2048:blk * 2048 + 1024, c0:c0 + (g1 - g0)]),
            waits=((("coll", cv),) if gi == 0 else ()), inc="kvasm", n=16)
        b.op("pool", lambda e, g0=g0, g1=g1, blk=blk, c0=c0: e.dma_start(
            out=vfullB[g0:g1, :],
            in_=kv_all[blk * 2048 + 1024 + c0:blk * 2048 + 1024 + c0 + (g1 - g0), :]),
            inc="kvasm", n=16)
    kva = b.v("kvasm")
    b.op("sync", lambda e: None, waits=(("kvasm", kva),))

    p.new_phase("B_")
    ioB = {"x": x0, "q": q_h, "kfull": kfullB, "vfull": vfullB,
           "x1": x1_h, "x2": x2_h, "h128_in": h128_in, "masks": masks_in}
    build_l0_rest(debug=False, p=p, io=ioB)

    p.new_phase("C_")
    ioC = {"x": x2_h, "q1": q1_h, "qgain": qg_in, "k1": kv1_own[0:512, :],
           "v1": kv1_own[512:1024, :].rearrange("a b -> (a b)").rearrange(
               "(t f) -> t f", f=512),
           "vhat": vhat_h}
    build_l1_qkv(debug=False, p=p, io=ioC)

    p.new_phase("G1_")
    b.op("pool", lambda e: e.collective_compute(
        "AllGather", ALU.bypass, replica_groups=RG_PAIRS,
        ins=[kv1_own[:, :]], outs=[kv1_all[:, :]]), inc="coll", n=16)
    cv = b.v("coll")
    for gi, (g0, g1, blk, c0) in enumerate(KSRC):
        b.op("pool", lambda e, g0=g0, g1=g1, blk=blk, c0=c0: e.dma_start(
            out=k1fullD[:, g0:g1],
            in_=kv1_all[blk * 1024:blk * 1024 + 512, c0:c0 + (g1 - g0)]),
            waits=((("coll", cv),) if gi == 0 else ()), inc="kvasm", n=16)
        b.op("pool", lambda e, g0=g0, g1=g1, blk=blk, c0=c0: e.dma_start(
            out=v1fullD[g0:g1, :],
            in_=kv1_all[blk * 1024 + 512:blk * 1024 + 1024, :].rearrange(
                "a b -> (a b)").rearrange("(t f) -> t f", f=512)[
                    c0:c0 + (g1 - g0), :]),
            inc="kvasm", n=16)
    kva = b.v("kvasm")
    b.op("sync", lambda e: None, waits=(("kvasm", kva),))

    p.new_phase("D_")
    ioD = {"x": x2_h, "q1": q1_h, "k1full": k1fullD, "v1full": v1fullD,
           "vhat": vhat_h, "x3": x3_h, "out": out_x, "h128_in": h128_in,
           "masks": masks_in, "qgain": qg_in}
    build_l1_rest(debug=False, p=p, io=ioD)
    return p


def kernel_fused(**inputs):
    inp = {k: np.asarray(v) for k, v in inputs.items()}
    f32 = np.float32
    cont = np.ascontiguousarray
    x_full = inp["L0_attn_hidden_states"].astype(f32)
    shared = {
        "wqT": cont(inp["L0_attn_Wq"].T.astype(f32)),
        "wkT": cont(inp["L0_attn_Wk"].T.astype(f32)),
        "wvT": cont(inp["L0_attn_Wv"].T.astype(f32)),
        "woT": cont(inp["L0_attn_Wo"].T.astype(f32)),
        "w1T": cont(inp["L0_ffn_W1"].T.astype(f32)),
        "w3T": cont(inp["L0_ffn_W3"].T.astype(f32)),
        "w2T": cont(inp["L0_ffn_W2"].T.astype(f32)),
        "lam": np.concatenate([
            inp["L0_attn_lambda_q1"], inp["L0_attn_lambda_k1"],
            inp["L0_attn_lambda_q2"], inp["L0_attn_lambda_k2"]])[None, :].astype(f32),
        "sublnw": inp["L0_attn_subln_weight"].reshape(128, 1).astype(f32),
        "wq": cont(inp["L1_attn_Wq"].astype(f32)),
        "wk": cont(inp["L1_attn_Wk"].astype(f32)),
        "wv": cont(inp["L1_attn_Wv"].astype(f32)),
        "qA": cont(inp["L1_attn_q_A"].astype(f32)),
        "qB": cont(inp["L1_attn_q_B"].astype(f32)),
        "kA": cont(inp["L1_attn_k_A"].astype(f32)),
        "kB": cont(inp["L1_attn_k_B"].astype(f32)),
        "vA": cont(inp["L1_attn_v_A"].astype(f32)),
        "vB": cont(inp["L1_attn_v_B"].astype(f32)),
        "qgain": inp["L1_attn_q_gain"][None, :].astype(f32),
        "wproj": cont(inp["L1_attn_Wproj"].astype(f32)),
        "pA": cont(inp["L1_attn_p_A"].astype(f32)),
        "pB": cont(inp["L1_attn_p_B"].astype(f32)),
        "wupT": cont(inp["L1_ffn_W_up"].T.astype(f32)),
        "wgateT": cont(inp["L1_ffn_W_gate"].T.astype(f32)),
        "wdownT": cont(inp["L1_ffn_W_down"].T.astype(f32)),
        "su_ug": cont(inp["L1_ffn_SU_ug"].reshape(16, 128).T.astype(f32)),
        "su_dp": cont(inp["L1_ffn_SU_dp"].reshape(64, 128).T.astype(f32)),
        "ws_up": cont(inp["L1_ffn_Wscale_ug"][:8192].reshape(64, 128).T.astype(f32)),
        "ws_gate": cont(inp["L1_ffn_Wscale_ug"][8192:].reshape(64, 128).T.astype(f32)),
        "ws_dp": cont(inp["L1_ffn_Wscale_dp"].reshape(16, 128).T.astype(f32)),
        "h128_in": _hadamard128(),
    }
    pos = {h: own_pos(h) for h in (0, 1)}
    r0 = {h: rope0_tables(h) for h in (0, 1)}
    r1 = {h: rope1_tables(h) for h in (0, 1)}
    mk = {h: np.ascontiguousarray(mask_tables(h).reshape(2048, 512))
          for h in (0, 1)}
    nc = _get_nc("F", lambda debug=False: build_fused())
    maps = []
    for c in range(8):
        bb, h = divmod(c, 2)
        m = dict(shared)
        m["x"] = cont(x_full[bb][pos[h]].T)
        m["rope0c"] = r0[h][0]; m["rope0s"] = r0[h][1]
        m["rope1c"] = r1[h][0]; m["rope1s"] = r1[h][1]
        m["masks"] = mk[h]
        maps.append(m)
    res = run_bass_kernel_spmd(nc, maps, core_ids=list(range(8)))
    out = np.empty((4, 2048, 2048), dtype=np.float32)
    for c in range(8):
        bb, h = divmod(c, 2)
        out[bb][pos[h]] = res.results[c]["out"].T
    return out



def emit_vproj_tm2(p, w_dram, spans, out_slab3, dout, vwbuf, xn_ready,
                   sem_prefix="v"):
    b = p.b
    nvch = dout // 512
    wb3 = vwbuf.rearrange("p (c n) -> p c n", c=NTILE)
    A = p.banks
    wsem = sem_prefix + "_wdma"
    evac = {}
    entry_act = b.v("act")
    entry_pe = b.v("pe")
    for nch in range(nvch):
        dma_w = []
        if nch == 0:
            dma_w.append(("pe", entry_pe))
        else:
            dma_w.append(("pe", evac_pe_last))
        b.op("pool", lambda e, nch=nch: e.dma_start(
            out=wb3[:, :, :], in_=chunked_cols(w_dram, nch * 512, 512)),
            waits=dma_w, inc=wsem, n=16)
        wv = b.v(wsem)
        for tt in range(16):
            xn3 = spans[tt // 8][0]
            lt = tt % 8
            bank = A[4 + tt % 2]
            pe_w = []
            if tt == 0:
                pe_w.append((wsem, wv))
                if nch == 0:
                    pe_w.append(("act", entry_act))
                    if xn_ready:
                        pe_w.append(xn_ready)
            key = (nch * 16 + tt) - 2
            if key >= 0:
                pe_w.append(("act", evac[key]))
            for kc in range(NTILE):
                b.op("pe", lambda e, xn3=xn3, lt=lt, kc=kc, bank=bank: e.matmul(
                    bank.ap()[:, :], xn3[:, kc, lt * 128:(lt + 1) * 128],
                    wb3[:, kc, :],
                    start=(kc == 0), stop=(kc == NTILE - 1)),
                    waits=pe_w if kc == 0 else (),
                    inc="pe" if kc == NTILE - 1 else None)
            pm = b.v("pe")
            b.op("act", lambda e, tt=tt, nch=nch, bank=bank: e.copy(
                out=out_slab3[:, tt, nch * 512:(nch + 1) * 512],
                in_=bank.ap()[:, :]),
                waits=(("pe", pm),), inc="act")
            evac[nch * 16 + tt] = b.v("act")
        evac_pe_last = b.v("pe")
    return ("act", b.v("act"))


def emit_l0_subln(p, attn_c, scr, sw, smalls, ready):
    b = p.b
    A = p.banks
    ac3 = attn_c.rearrange("p (c t) -> p c t", c=NTILE)
    sm = smalls
    prev_pe = 0
    prev_stt = 0
    for h in range(16):
        w = list(ready) if h == 0 else [("pe", prev_pe)]
        if h >= 1:
            w.append(("dve", prev_stt))
        b.op("act", lambda e, h=h: e.activation(
            out=scr, in_=ac3[:, h, :], func=AF.Square), waits=w, inc="act")
        am = b.v("act")
        dw = b.v("dve")
        for tch in range(4):
            b.op("pe", lambda e, tch=tch: e.matmul(
                A[tch].ap()[0:1, :], p.c_ones_col.ap()[:, 0:1],
                scr[:, tch * 512:(tch + 1) * 512], start=True, stop=True),
                waits=((("act", am), ("dve", dw)) if tch == 0 else ()),
                inc="pe" if tch == 3 else None)
        prev_pe = b.v("pe")
        for half in range(2):
            for tch in range(2):
                b.op("dve", lambda e, half=half, tch=tch: e.tensor_scalar(
                    sm[0:1, tch * 512:(tch + 1) * 512],
                    A[2 * half + tch].ap()[0:1, :],
                    1.0 / 128.0, 1e-5, ALU.mult, ALU.add),
                    waits=((("pe", prev_pe),) if (half == 0 and tch == 0)
                           else ()),
                    inc="dve" if tch == 1 else None)
            dm = b.v("dve")
            b.op("act", lambda e: e.activation(out=sm[0:1, 0:1024],
                                               in_=sm[0:1, 0:1024],
                                               func=AF.Sqrt),
                 waits=(("dve", dm),), inc="act")
            am2 = b.v("act")
            b.op("dve", lambda e: e.reciprocal(sm[0:1, 0:1024],
                                               sm[0:1, 0:1024]),
                 waits=(("act", am2),), inc="dve")
            dm = b.v("dve")
            b.op("dve", lambda e, half=half: e.tensor_copy(
                p.ln_small.ap()[0:1, half * 1024:(half + 1) * 1024],
                sm[0:1, 0:1024]), waits=(("dve", dm),), inc="dve")
        dm = b.v("dve")
        for ch in range(4):
            b.op("pe", lambda e, ch=ch: e.matmul(
                A[4 + ch].ap()[:, :], p.c_ones_row.ap()[0:1, :],
                p.ln_small.ap()[0:1, ch * 512:(ch + 1) * 512],
                start=True, stop=True),
                waits=((("dve", dm),) if ch == 0 else ()),
                inc="pe" if ch == 3 else None)
        prev_pe = b.v("pe")
        for ch in range(4):
            b.op("dve", lambda e, h=h, ch=ch: e.scalar_tensor_tensor(
                ac3[:, h, ch * 512:(ch + 1) * 512],
                ac3[:, h, ch * 512:(ch + 1) * 512], sw[:, 0:1],
                A[4 + ch].ap()[:, :], ALU.mult, ALU.mult),
                waits=((("pe", prev_pe),) if ch == 0 else ()),
                inc="dve" if ch == 3 else None)
        prev_stt = b.v("dve")
    return b.v("dve")


def emit_outproj2(p, w_dram, src3, x_dram, x1_dram, wbuf, xold, src_ready,
                  spans_cb, sem_prefix="o"):
    b = p.b
    wb3 = wbuf.ap()[:, 0:2 * NTILE * 128].rearrange("p (c n) -> p c n",
                                                    c=2 * NTILE)
    xo3 = xold.rearrange("p (c t) -> p c t", c=2)
    A = p.banks
    evac = {}
    entry_act = b.v("act")
    entry_pe = b.v("pe")
    entry_dve = b.v("dve")
    x1r3 = p.x1ring.rearrange("p (c t) -> p c t", c=2)
    for dt in range(NTILE):
        buf = dt % 2
        wsem = "%s_w%d" % (sem_prefix, buf)
        xsem = "%s_x%d" % (sem_prefix, buf)
        w = [("pe", entry_pe)] if dt < 2 else [("pe", evac[dt - 2][0])]
        b.op("pool", lambda e, dt=dt, buf=buf: e.dma_start(
            out=wb3[:, buf * NTILE:(buf + 1) * NTILE, :],
            in_=chunked_cols(w_dram, dt * 128, 128)),
            waits=w, inc=wsem, n=16)
        wv = b.v(wsem)
        if dt >= 2:
            wx = [("dve", evac[dt - 2][1])]
        else:
            wx = [src_ready] if src_ready else []
        b.op("sync", lambda e, dt=dt, buf=buf: e.dma_start(
            out=xo3[:, buf, :], in_=x_dram.rearrange(
                "(c p) t -> c p t", p=128)[dt, :, :]),
            waits=wx, inc=xsem, n=16)
        xv = b.v(xsem)
        pe_w = [(wsem, wv)]
        if dt == 0 and src_ready:
            pe_w.append(src_ready)
        if dt < 2:
            pe_w.append(("act", entry_act))
            pe_w.append(("dve", entry_dve))
        if dt >= 2:
            pe_w.append(("dve", evac[dt - 2][1]))
        for sp, cb in enumerate(spans_cb):
            for tch in range(2):
                bank = A[(4 * buf + 2 * sp + tch) % 8]
                for kc in range(NTILE):
                    b.op("pe", lambda e, cb=cb, tch=tch, kc=kc, bank=bank,
                         buf=buf: e.matmul(
                        bank.ap()[:, :], wb3[:, buf * NTILE + kc, :],
                        src3[:, kc, cb + tch * 512:cb + (tch + 1) * 512],
                        start=(kc == 0), stop=(kc == NTILE - 1)),
                        waits=pe_w if (sp == 0 and tch == 0 and kc == 0) else (),
                        inc="pe" if (sp == len(spans_cb) - 1 and tch == 1 and
                                     kc == NTILE - 1) else None)
        pm = b.v("pe")
        for sp, cb in enumerate(spans_cb):
            for tch in range(2):
                bank = A[(4 * buf + 2 * sp + tch) % 8]
                ww = [("pe", pm), (xsem, xv)] if (sp == 0 and tch == 0) else []
                if dt >= 2 and sp == 0 and tch == 0:
                    ww.append(("spill", evac[dt - 2][2]))
                b.op("dve", lambda e, dt=dt, cb=cb, tch=tch, buf=buf,
                     bank=bank: e.tensor_add(
                    x1r3[:, buf, cb + tch * 512:cb + (tch + 1) * 512],
                    bank.ap()[:, :],
                    xo3[:, buf, cb + tch * 512:cb + (tch + 1) * 512]),
                    waits=ww,
                    inc="dve" if (sp == len(spans_cb) - 1 and tch == 1)
                    else None)
        dm = b.v("dve")
        b.op("sync", lambda e, dt=dt, buf=buf: e.dma_start(
            out=x1_dram.rearrange("(c p) t -> c p t", p=128)[dt, :, :],
            in_=x1r3[:, buf, :]), waits=(("dve", dm),), inc="spill", n=16)
        evac[dt] = (pm, dm, b.v("spill"))
    return b.v("dve"), b.v("spill")


def emit_l1_kvq2(p, xn_spans, xn_own3, wq, wk, wv, qA, qB_sb, kA, kB_sb, vA,
                 vB_sb, qgain_ap, qring3, q1_dram, kring3, k1_dram,
                 vtring3, v1_dram, vf_own3, vhring3, vhat_dram,
                 uq, uk2, uv2, uv_own, uv_dram, dynblk,
                 wbuf, vwbuf, xn_own_ready, xn_ready, rope1_own, rope1_glob):
    b = p.b
    A = p.banks
    wb3 = wbuf.ap()[:, 0:2 * NTILE * 128].rearrange("p (c n) -> p c n",
                                                    c=2 * NTILE)
    vwb3 = vwbuf.rearrange("p (c n) -> p c n", c=NTILE)
    qt3 = sb3(p.qtmp, 2, T)
    sm = p.l1small
    sq3 = p.l1sq.rearrange("p (c t) -> p c t", c=2)
    pre3 = p.l1pre.rearrange("p (c t) -> p c t", c=2)

    um_q = emit_lowrank_u(p, qA, xn_own3, uq, xn_own_ready, "uq", A[6],
                          is_view=True)
    for sp, (xn3, cb) in enumerate(xn_spans):
        emit_lowrank_u(p, kA, xn3, uk2[:, cb:cb + 1024],
                       xn_ready if sp == 0 else None, "uk%d" % sp, A[6],
                       is_view=True)
    um_k = b.v("act")
    for sp, (xn3, cb) in enumerate(xn_spans):
        emit_lowrank_u(p, vA, xn3, uv2[:, cb:cb + 1024], None, "uv%d" % sp,
                       A[6], is_view=True)
    um_v = b.v("act")
    b.op("sync", lambda e: e.dma_start(out=uv_dram, in_=uv2),
         waits=(("act", um_v),), inc="uvd", n=16)
    uvd = b.v("uvd")
    b.op("pool", lambda e: e.dma_start(
        out=uv_own[0:16, 0:512], in_=uv_dram[0:16, bass.ts(dynblk["p0"], 512)]),
        waits=(("uvd", uvd),), inc="uvo", n=16)
    b.op("pool", lambda e: e.dma_start(
        out=uv_own[0:16, 512:1024],
        in_=uv_dram[0:16, bass.ts(dynblk["p1"], 512)]), inc="uvo", n=16)
    uvo = b.v("uvo")

    def head_norm_rope2(dst, buf, bank_pair, gain_slice, evacs, tabs,
                        ring_wait=None):
        pm = b.v("pe")
        for tch in range(2):
            b.op("act", lambda e, tch=tch, buf=buf: e.copy(
                out=qt3[:, buf, tch * 512:(tch + 1) * 512],
                in_=bank_pair[tch].ap()[:, :]),
                waits=([("pe", pm)] + evacs if tch == 0 else ()),
                inc="act" if tch == 1 else None)
        b.op("act", lambda e, buf=buf: e.activation(
            out=sq3[:, buf, :], in_=qt3[:, buf, :], func=AF.Square),
            inc="act")
        am2 = b.v("act")
        dw = b.v("dve")
        for tch in range(2):
            b.op("pe", lambda e, tch=tch, buf=buf: e.matmul(
                A[4 + tch].ap()[0:1, :], p.c_ones_col.ap()[:, 0:1],
                sq3[:, buf, tch * 512:(tch + 1) * 512],
                start=True, stop=True),
                waits=((("act", am2), ("dve", dw)) if tch == 0 else ()),
                inc="pe" if tch == 1 else None)
        pm2 = b.v("pe")
        for tch in range(2):
            b.op("dve", lambda e, tch=tch: e.tensor_scalar(
                sm[0:1, tch * 512:(tch + 1) * 512], A[4 + tch].ap()[0:1, :],
                1.0 / 128.0, 1.1920929e-07, ALU.mult, ALU.add),
                waits=((("pe", pm2),) if tch == 0 else ()),
                inc="dve" if tch == 1 else None)
        dm = b.v("dve")
        b.op("act", lambda e: e.activation(out=sm[0:1, 0:1024],
                                           in_=sm[0:1, 0:1024], func=AF.Sqrt),
             waits=(("dve", dm),), inc="act")
        am3 = b.v("act")
        b.op("dve", lambda e: e.reciprocal(sm[0:1, 0:1024], sm[0:1, 0:1024]),
             waits=(("act", am3),), inc="dve")
        dm = b.v("dve")
        if gain_slice is not None:
            b.op("dve", lambda e: e.tensor_scalar(
                p.ln_small.ap()[0:1, 0:1024], sm[0:1, 0:1024], gain_slice,
                None, ALU.mult), waits=(("dve", dm),), inc="dve")
        else:
            b.op("dve", lambda e: e.tensor_copy(p.ln_small.ap()[0:1, 0:1024],
                                                sm[0:1, 0:1024]),
                 waits=(("dve", dm),), inc="dve")
        dm = b.v("dve")
        for tch in range(2):
            b.op("pe", lambda e, tch=tch: e.matmul(
                A[6 + tch].ap()[:, :], p.c_ones_row.ap()[0:1, :],
                p.ln_small.ap()[0:1, tch * 512:(tch + 1) * 512],
                start=True, stop=True),
                waits=((("dve", dm),) if tch == 0 else ()),
                inc="pe" if tch == 1 else None)
        pm3 = b.v("pe")
        for tch in range(2):
            w = []
            if tch == 0:
                w.append(("pe", pm3))
                if ring_wait:
                    w.append(ring_wait)
            b.op("dve", lambda e, tch=tch, buf=buf: e.tensor_mul(
                dst[64:128, tch * 512:(tch + 1) * 512],
                qt3[:, buf, tch * 512:(tch + 1) * 512][64:128, :],
                A[6 + tch].ap()[64:128, :]), waits=w)
            b.op("dve", lambda e, tch=tch, buf=buf: e.tensor_mul(
                pre3[:, buf, tch * 512:(tch + 1) * 512][0:64, :],
                qt3[:, buf, tch * 512:(tch + 1) * 512][0:64, :],
                A[6 + tch].ap()[0:64, :]),
                inc="dve" if tch == 1 else None)
        src = pre3[:, buf, :]
        C, SS = tabs
        tmp = p.ropetmp.ap()
        dmr = b.v("dve")
        b.op("dve", lambda e: e.tensor_mul(dst[0:32, :], src[0:32, :],
                                           C[0:32, :]), waits=(("dve", dmr),))
        b.op("dve", lambda e: e.tensor_mul(tmp[0:32, :], src[32:64, :],
                                           SS[32:64, :]))
        b.op("dve", lambda e: e.tensor_mul(dst[32:64, :], src[32:64, :],
                                           C[32:64, :]))
        b.op("dve", lambda e: e.tensor_mul(tmp[32:64, :], src[0:32, :],
                                           SS[0:32, :]), inc="dve")
        dmr = b.v("dve")
        b.op("dve", lambda e: e.tensor_add(dst[0:32, :], dst[0:32, :],
                                           tmp[0:32, :]),
             waits=(("dve", dmr),))
        b.op("dve", lambda e: e.tensor_add(dst[32:64, :], dst[32:64, :],
                                           tmp[32:64, :]), inc="dve")
        return b.v("dve")

    evac = {}
    entry_pe = b.v("pe")
    for dt in range(NTILE):
        buf = dt % 2
        psem = "q1_w%d" % buf
        w = [("pe", entry_pe)] if dt < 2 else [("pe", evac[dt - 2][0])]
        b.op("pool", lambda e, dt=dt, buf=buf: e.dma_start(
            out=wb3[:, buf * NTILE:(buf + 1) * NTILE, :],
            in_=chunked_cols(wq, dt * 128, 128)),
            waits=w, inc=psem, n=16)
        wv_ = b.v(psem)
        pe_w = [(psem, wv_)]
        if dt == 0:
            pe_w.append(("act", um_q))
        if dt >= 2:
            pe_w.append(("act", evac[dt - 2][1]))
        for tch in range(2):
            bank = A[2 * buf + tch]
            for kc in range(NTILE):
                b.op("pe", lambda e, buf=buf, tch=tch, kc=kc, bank=bank:
                     e.matmul(
                    bank.ap()[:, :], wb3[:, buf * NTILE + kc, :],
                    xn_own3[:, kc, tch * 512:(tch + 1) * 512],
                    start=(kc == 0), stop=False),
                    waits=pe_w if (tch == 0 and kc == 0) else ())
            b.op("pe", lambda e, dt=dt, tch=tch, bank=bank: e.matmul(
                bank.ap()[:, :], qB_sb[0:16, dt * 128:(dt + 1) * 128],
                uq[0:16, tch * 512:(tch + 1) * 512],
                start=False, stop=True),
                inc="pe" if tch == 1 else None)
        pm = b.v("pe")
        evacs = [("dve", evac[dt - 2][2])] if dt >= 2 else []
        rw = ("q1o", 16 * (dt - 1)) if dt >= 2 else None
        dm = head_norm_rope2(qring3[:, buf, :], buf, (A[2 * buf], A[2 * buf + 1]),
                             qgain_ap[0:1, dt:dt + 1], evacs, rope1_own,
                             ring_wait=rw)
        b.op("sync", lambda e, dt=dt, buf=buf: e.dma_start(
            out=q1_dram[dt * 128:(dt + 1) * 128, :], in_=qring3[:, buf, :]),
            waits=(("dve", dm),), inc="q1o", n=16)
        evac[dt] = (pm, b.v("act"), dm)
    qo_all = b.v("q1o")

    evk = {}
    entry_pe = b.v("pe")
    for dt in range(4):
        psem = "k1_w%d" % (dt % 2)
        w = [("pe", entry_pe)] if dt < 2 else [("pe", evk[dt - 2][0])]
        b.op("pool", lambda e, dt=dt: e.dma_start(
            out=wb3[:, (dt % 2) * NTILE:((dt % 2) + 1) * NTILE, :],
            in_=chunked_cols(wk, dt * 128, 128)),
            waits=w, inc=psem, n=16)
        wv_ = b.v(psem)
        dms = []
        for sp, (xn3, cb) in enumerate(xn_spans):
            pe_w = [(psem, wv_)]
            if dt == 0 and sp == 0:
                pe_w.append(("act", um_k))
            if dt >= 1 or sp >= 1:
                pe_w.append(("act", b.v("act")))
            for tch in range(2):
                bank = A[2 * sp + tch]
                for kc in range(NTILE):
                    b.op("pe", lambda e, dt=dt, xn3=xn3, tch=tch, kc=kc,
                         bank=bank: e.matmul(
                        bank.ap()[:, :],
                        wb3[:, (dt % 2) * NTILE + kc, :],
                        xn3[:, kc, tch * 512:(tch + 1) * 512],
                        start=(kc == 0), stop=False),
                        waits=pe_w if (tch == 0 and kc == 0) else ())
                b.op("pe", lambda e, dt=dt, cb=cb, tch=tch, bank=bank: e.matmul(
                    bank.ap()[:, :], kB_sb[0:16, dt * 128:(dt + 1) * 128],
                    uk2[0:16, cb + tch * 512:cb + (tch + 1) * 512],
                    start=False, stop=True),
                    inc="pe" if tch == 1 else None)
            rw = ("k1o", 16 * (dt - 1)) if (dt >= 2 and sp == 0) else None
            dm = head_norm_rope2(
                kring3[:, dt % 2, cb:cb + 1024], sp, (A[2 * sp], A[2 * sp + 1]),
                None, [("dve", b.v("dve"))] if (dt + sp) else [],
                (rope1_glob[0][:, cb:cb + 1024], rope1_glob[1][:, cb:cb + 1024]),
                ring_wait=rw)
            dms.append(dm)
        b.op("sync", lambda e, dt=dt: e.dma_start(
            out=k1_dram[dt * 128:(dt + 1) * 128, :], in_=kring3[:, dt % 2, :]),
            waits=(("dve", dms[-1]),), inc="k1o", n=16)
        evk[dt] = (b.v("pe"), b.v("act"), b.v("dve"))
    ko_all = b.v("k1o")

    entry_pe = b.v("pe")
    entry_act = b.v("act")
    b.op("pool", lambda e: e.dma_start(
        out=vwb3, in_=chunked_cols(wv, 0, 512)),
        waits=(("pe", entry_pe),), inc="v1_wdma", n=16)
    wv_ = b.v("v1_wdma")
    evacm = {}
    for tt in range(16):
        xn3, cb = xn_spans[tt // 8]
        lt = tt % 8
        bank = A[4 + tt % 2]
        pe_w = []
        if tt == 0:
            pe_w.append(("v1_wdma", wv_))
            pe_w.append(("act", entry_act))
        if tt >= 2:
            pe_w.append(("act", evacm[tt - 2]))
        for kc in range(NTILE):
            b.op("pe", lambda e, xn3=xn3, lt=lt, kc=kc, bank=bank: e.matmul(
                bank.ap()[:, :], xn3[:, kc, lt * 128:(lt + 1) * 128],
                vwb3[:, kc, :], start=(kc == 0), stop=False),
                waits=pe_w if kc == 0 else ())
        b.op("pe", lambda e, cb=cb, lt=lt, bank=bank: e.matmul(
            bank.ap()[:, :], uv2[0:16, cb + lt * 128:cb + (lt + 1) * 128],
            vB_sb[0:16, :], start=False, stop=True), inc="pe")
        pm = b.v("pe")
        aw = [("pe", pm)]
        if tt >= 2:
            aw.append(("v1o", 16 * (tt - 1)))
        b.op("act", lambda e, tt=tt, bank=bank: e.copy(
            out=vtring3[:, tt % 2, :], in_=bank.ap()[:, :]),
            waits=aw, inc="act")
        evacm[tt] = b.v("act")
        b.op("sync", lambda e, tt=tt: e.dma_start(
            out=v1_dram[tt * 128:(tt + 1) * 128, :],
            in_=vtring3[:, tt % 2, :]),
            waits=(("act", evacm[tt]),), inc="v1o", n=16)
    vo_all = b.v("v1o")

    evac2 = {}
    for dt in range(4):
        bank0, bank1 = A[6], A[7]
        pe_w = [("uvo", uvo)] if dt == 0 else [("act", evac2[dt - 1])]
        for tch in range(2):
            bank = (bank0, bank1)[tch]
            for kc in range(NTILE):
                b.op("pe", lambda e, dt=dt, tch=tch, kc=kc, bank=bank: e.matmul(
                    bank.ap()[:, :], vwb3[:, kc, dt * 128:(dt + 1) * 128],
                    xn_own3[:, kc, tch * 512:(tch + 1) * 512],
                    start=(kc == 0), stop=False),
                    waits=pe_w if (tch == 0 and kc == 0) else ())
            b.op("pe", lambda e, dt=dt, tch=tch, bank=bank: e.matmul(
                bank.ap()[:, :], vB_sb[0:16, dt * 128:(dt + 1) * 128],
                uv_own[0:16, tch * 512:(tch + 1) * 512],
                start=False, stop=True), inc="pe" if tch == 1 else None)
        pm = b.v("pe")
        for tch in range(2):
            bank = (bank0, bank1)[tch]
            b.op("act", lambda e, dt=dt, tch=tch, bank=bank: e.copy(
                out=vf_own3[:, dt, tch * 512:(tch + 1) * 512],
                in_=bank.ap()[:, :]),
                waits=((("pe", pm),) if tch == 0 else ()),
                inc="act" if tch == 1 else None)
        evac2[dt] = b.v("act")
        b.op("act", lambda e, dt=dt: e.activation(
            out=sq3[:, 0, :], in_=vf_own3[:, dt, :], func=AF.Square),
            inc="act")
        am2 = b.v("act")
        dw = b.v("dve")
        for tch in range(2):
            b.op("pe", lambda e, tch=tch: e.matmul(
                A[4 + tch].ap()[0:1, :], p.c_ones_col.ap()[:, 0:1],
                sq3[:, 0, tch * 512:(tch + 1) * 512],
                start=True, stop=True),
                waits=((("act", am2), ("dve", dw)) if tch == 0 else ()),
                inc="pe" if tch == 1 else None)
        pm2 = b.v("pe")
        b.op("act", lambda e: e.activation(out=sm[0:1, 0:512],
                                           in_=A[4].ap()[0:1, :], func=AF.Sqrt),
             waits=(("pe", pm2),))
        b.op("act", lambda e: e.activation(out=sm[0:1, 512:1024],
                                           in_=A[5].ap()[0:1, :], func=AF.Sqrt),
             inc="act")
        am3 = b.v("act")
        b.op("dve", lambda e: e.tensor_scalar_max(sm[0:1, 0:1024],
                                                  sm[0:1, 0:1024], 1e-12),
             waits=(("act", am3),), inc="dve")
        dm = b.v("dve")
        b.op("dve", lambda e: e.reciprocal(sm[0:1, 0:1024], sm[0:1, 0:1024]),
             waits=(("dve", dm),), inc="dve")
        dm = b.v("dve")
        b.op("dve", lambda e: e.tensor_copy(p.ln_small.ap()[0:1, 0:1024],
                                            sm[0:1, 0:1024]),
             waits=(("dve", dm),), inc="dve")
        dm = b.v("dve")
        for tch in range(2):
            b.op("pe", lambda e, tch=tch: e.matmul(
                A[4 + tch].ap()[:, :], p.c_ones_row.ap()[0:1, :],
                p.ln_small.ap()[0:1, tch * 512:(tch + 1) * 512],
                start=True, stop=True),
                waits=((("dve", dm),) if tch == 0 else ()),
                inc="pe" if tch == 1 else None)
        pm3 = b.v("pe")
        for tch in range(2):
            w = []
            if tch == 0:
                w.append(("pe", pm3))
                if dt >= 2:
                    w.append(("vh1o", 16 * (dt - 1)))
            b.op("dve", lambda e, dt=dt, tch=tch: e.tensor_mul(
                vhring3[:, dt % 2, tch * 512:(tch + 1) * 512],
                vf_own3[:, dt, tch * 512:(tch + 1) * 512],
                A[4 + tch].ap()[:, :]),
                waits=w, inc="dve" if tch == 1 else None)
        dm = b.v("dve")
        b.op("sync", lambda e, dt=dt: e.dma_start(
            out=vhat_dram[dt * 128:(dt + 1) * 128, :],
            in_=vhring3[:, dt % 2, :]),
            waits=(("dve", dm),), inc="vh1o", n=16)
    vho_all = b.v("vh1o")
    return qo_all, ko_all, vo_all, vho_all



I32 = mybir.dt.int32

PASS_SLOTS = [
    [(0, 0, list(range(4)), {r: r for r in range(4)}),
     (512, 512, list(range(8)), {4 + r: r for r in range(4)})],
    [(0, 1024, list(range(12)), {8 + r: r for r in range(4)}),
     (512, 1536, list(range(16)), {12 + r: r for r in range(4)})],
]


def build_fused2(debug=False, only_ab=False, only_abc=False):
    p = Prog("mkf2", phased=True)
    b = p.b
    nc = p.nc
    p.make_common(rope=True)

    x0 = p.din("x", [D, S], F32)
    wqT = p.din("wqT", [D, D], F32)
    wkT = p.din("wkT", [D, 1024], F32)
    wvT = p.din("wvT", [D, 1024], F32)
    woT = p.din("woT", [D, D], F32)
    w1T = p.din("w1T", [D, 8192], F32)
    w3T = p.din("w3T", [D, 8192], F32)
    w2T = p.din("w2T", [8192, D], F32)
    lam_in = p.din("lam", [1, 256], F32)
    subln_in = p.din("sublnw", [128, 1], F32)
    r0c_in = p.din("rope0cG", [128, S], BF16)
    r0s_in = p.din("rope0sG", [128, S], BF16)
    m0_in = p.din("masks0", [512, 512], BF16)
    h128_in = p.din("h128_in", [128, 128], BF16)
    wq1 = p.din("wq", [D, D], F32)
    wk1 = p.din("wk", [D, 512], F32)
    wv1 = p.din("wv", [D, 512], F32)
    qA = p.din("qA", [D, 16], F32)
    qB = p.din("qB", [16, D], F32)
    kA = p.din("kA", [D, 16], F32)
    kB = p.din("kB", [16, 512], F32)
    vA = p.din("vA", [D, 16], F32)
    vB = p.din("vB", [16, 512], F32)
    qg_in = p.din("qgain", [1, 16], F32)
    r1co_in = p.din("rope1cO", [64, T], BF16)
    r1so_in = p.din("rope1sO", [64, T], BF16)
    r1cg_in = p.din("rope1cG", [64, S], BF16)
    r1sg_in = p.din("rope1sG", [64, S], BF16)
    ob_in = p.din("ownblk", [1, 4], I32)
    m1_in = p.din("masks", [2048, 512], BF16)
    wproj = p.din("wproj", [D, D], F32)
    pA = p.din("pA", [D, 16], F32)
    pB = p.din("pB", [16, D], F32)
    wupT = p.din("wupT", [D, 8192], F32)
    wgateT = p.din("wgateT", [D, 8192], F32)
    wdownT = p.din("wdownT", [8192, D], F32)
    su_ug = p.din("su_ug", [128, 16], F32)
    su_dp = p.din("su_dp", [128, 64], F32)
    ws_up = p.din("ws_up", [128, 64], F32)
    ws_gate = p.din("ws_gate", [128, 64], F32)
    ws_dp = p.din("ws_dp", [128, 16], F32)
    out_x = p.dout("out", [D, T], F32)

    if debug:
        dr = lambda n, s, dt: nc.dram_tensor(
            n, s, dt, kind="ExternalOutput").ap()
    else:
        dr = lambda n, s, dt: nc.dram_tensor(n, s, dt, kind="Internal").ap()
    q_h = dr("qh", [D, S], BF16)
    x1f = dr("x1f", [D, S], F32)
    x2f = dr("x2f", [D, S], F32)
    xn_d = dr("xnd", [D, S], BF16)
    uv_d = dr("uvd", [16, S], BF16)
    x2own = dr("x2own", [D, T], F32)
    xnown_d = dr("xnownd", [D, T], BF16)
    q1_h = dr("q1h", [D, T], BF16)
    k1f = dr("k1f", [512, S], BF16)
    v1f = dr("v1f", [S, 512], BF16)
    vhat_h = dr("vhath", [512, T], BF16)
    x3_h = dr("x3h", [D, T], F32)

    p.phase = "AB_"
    p.c_h128 = p.sb("h128", [128, 128], BF16)
    abh = p.c_h128.ap()
    negBs = p.sb("negBs", [128, 32], F32)
    lnb_t = p.sb("lnb", [128, 1], F32)
    sw_t = p.sb("sw", [128, 1], F32)
    smax = p.sb("smax", [1, 256], F32)
    lsc_t = p.sb("lsc", [1, 256], F32)
    arena = p.sb("arena", [128, 99328], BF16)
    ar = arena.ap()
    arf = ar.bitcast(F32)
    xn = ar[:, 0:32768]
    xt = arf[:, 16384:32768]
    stats = arf[0:1, 32768:35840]
    qring = ar[:, 65536:69632]
    rbuf = ar[:, 71680:73728]
    tabC = ar[:, 73728:75776]
    tabS = ar[:, 75776:77824]
    vwbuf = ar[:, 77824:86016]
    wbuf = ar[:, 86016:94208]
    ks = ar[:, 32768:49152]
    vs = ar[:, 49152:65536]
    ac = ar[:, 0:32768]
    qhalf = ar[:, 65536:81920]
    masksL0 = ar[:, 81920:83968]
    ebuf = ar[:, 83968:86016]
    b12s = ar[:, 94208:95232]
    osbuf = arf[:, 47616:48640]
    smalls = arf[0:1, 48640:49664]
    op_xold = arf[:, 16384:20480]
    p.x1ring = arf[:, 20480:24576]
    xt2 = arf[:, 0:16384]
    stats4 = arf[0:1, 24576:27648]
    rbuf2 = ar[:, 55296:57344]
    hidden = ar[:, 0:65536]
    xn2 = ar[:, 65536:81920]
    fwht_sc = arf[:, 32768:40960]
    w2buf = ar[:, 65536:73728]
    x2ring = arf[:, 40960:43008]
    f_xold = arf[:, 47104:49152]
    silu_b = ar[:, 98304:99328]

    xnf3 = xn.rearrange("p (c t) -> p c t", c=NTILE)
    xt3 = xt.rearrange("p (c t) -> p c t", c=NTILE)
    spans = [(xnf3[:, :, 0:1024], 0), (xnf3[:, :, 1024:2048], 1024)]

    b.op("sync", lambda e: e.dma_start(out=tabC, in_=r0c_in), inc="adma", n=16)
    b.op("sync", lambda e: e.dma_start(out=tabS, in_=r0s_in), inc="adma", n=16)
    b.op("sync", lambda e: e.dma_start(out=abh[:, :], in_=h128_in),
         inc="adma", n=16)

    ln_m = {}
    adv = b.v("adma")
    for H in (0, 1):
        lsem = "lx%d" % H
        w = [("dve", ln_m[0])] if H == 1 else []
        b.op("sync", lambda e, H=H: e.dma_start(
            out=xt3, in_=x0[:, H * 1024:(H + 1) * 1024].rearrange(
                "(c p) t -> p c t", p=128)), waits=w, inc=lsem, n=16)
        lv = b.v(lsem)
        entry = ((lsem, lv), ("adma", adv)) if H == 0 else ((lsem, lv),)
        ln_m[H] = emit_layernorm(p, xt3, xnf3[:, :, H * 1024:(H + 1) * 1024],
                                 stats, rbuf, entry=entry)
    lam_m = emit_l0_lambda(p, lam_in, subln_in, lsc_t.ap(), lnb_t.ap(),
                           sw_t.ap())

    def q_ring_dma(dt, slot3, wait):
        for r in (0, 1):
            j = 2 * dt + r
            t_, base = qpos(j)
            s = 2 * t_ + base // 64
            b.op("sync", lambda e, s=s, r=r, slot3=slot3: e.dma_start(
                out=q_h[64 * s:64 * s + 64, :], in_=slot3[64 * r:64 * r + 64, :]),
                waits=(wait,) if r == 0 else (), inc="qo", n=16)
    rope_tabs = [(tabC[:, 0:1024], tabS[:, 0:1024]),
                 (tabC[:, 1024:2048], tabS[:, 1024:2048])]
    qm = emit_proj_fm(p, wqT, None, qring, D, _T(wbuf), ("dve", ln_m[1]),
                      rope="l0", sem_prefix="q", spans=spans, out_w=2048,
                      rope_tabs=rope_tabs, ring_out=(2, 32, q_ring_dma))
    km = emit_proj_fm(p, wkT, None, ks, 1024, _T(wbuf), None, rope="l0",
                      sem_prefix="k", spans=spans, out_w=2048,
                      rope_tabs=rope_tabs)
    vm = emit_vproj_tm2(p, wvT, spans, sb3_like(vs, NTILE), 1024, vwbuf,
                        ("dve", ln_m[1]))
    if debug:
        dbg_xn = p.dout("dbg_xn", [D, S], BF16)
        dbg_k = p.dout("dbg_k", [1024, S], BF16)
        dbg_v = p.dout("dbg_v", [S, 1024], BF16)
        b.op("sync", lambda e: e.dma_start(
            out=dbg_xn.rearrange("(c p) t -> p c t", p=128), in_=xnf3),
            waits=(("dve", ln_m[1]),), inc="dbg", n=16)
        b.op("sync", lambda e: e.dma_start(
            out=dbg_k.rearrange("(c p) t -> p c t", p=128),
            in_=sb3_like(ks, 8)), waits=(("dve", km[1]),), inc="dbg", n=16)
        b.op("sync", lambda e: e.dma_start(
            out=dbg_v.rearrange("(c p) t -> p c t", p=128),
            in_=sb3_like(vs, NTILE)), waits=(("act", vm[1]),), inc="dbg", n=16)

    if debug:
        for tag, wsem, wval in (("a", "adma", adv), ("b", "dve", ln_m[1]),
                                ("c", "dve", km[1])):
            dh = p.dout("dbg_h128_%s" % tag, [128, 128], BF16)
            b.op("sync", lambda e, dh=dh: e.dma_start(
                out=dh, in_=abh[:, :]),
                waits=((wsem, wval),), inc="dbg", n=16)
    b.op("sync", lambda e: e.dma_start(
        out=masksL0.rearrange("p (c t) -> p c t", c=4),
        in_=m0_in.rearrange("(c p) t -> p c t", p=128)),
        waits=(("act", vm[1]),), inc="mld", n=16)
    mlv = b.v("mld")

    if debug:
        dh = p.dout("dbg_h128_e", [128, 128], BF16)
        b.op("sync", lambda e, dh=dh: e.dma_start(
            out=dh, in_=abh[:, :]),
            waits=(("act", vm[1]), ("mld", mlv)), inc="dbg", n=16)
    attn_m = 0
    for ps in (0, 1):
        qw = [("qo", qm[1]), ("dve", km[1]), ("act", vm[1])] if ps == 0 \
            else [("dve", attn_m)]
        b.op("sync", lambda e, ps=ps: e.dma_start(
            out=sb3_like(qhalf, NTILE),
            in_=q_h[:, ps * 1024:(ps + 1) * 1024].rearrange(
                "(c p) t -> p c t", p=128)), waits=qw, inc="qld", n=16)
        qlv = b.v("qld")
        bready = [("qld", qlv), ("dve", km[1]), ("act", vm[1]),
                  ("adma", b.v("adma"))]
        if ps == 1:
            bready.append(("pe", b.v("pe")))
        bm = emit_l0_bounds(p, qhalf, ks, ebuf, smax, negBs, ready=bready)
        ready = [("qld", qlv), ("act", max(bm, lam_m)), ("mld", mlv)]
        if debug:
            dh = p.dout("dbg_h128_f%d" % ps, [128, 128], BF16)
            b.op("sync", lambda e, dh=dh: e.dma_start(
                out=dh, in_=abh[:, :]),
                waits=(("act", max(bm, lam_m)), ("qld", qlv)), inc="dbg", n=16)
            win2 = nc.alloc_sbuf_tensor_at("dbgwin2", [128, 6512], BF16,
                                           offset=16512) if ps == 0 else None
            if ps == 0:
                dwf0 = p.dout("dbg_winf0", [128, 6512], BF16)
                b.op("sync", lambda e, dwf0=dwf0, win2=win2: e.dma_start(
                    out=dwf0, in_=win2.ap()[:, :]),
                    waits=(("act", max(bm, lam_m)), ("qld", qlv)),
                    inc="dbg", n=16)
                dwa1 = p.dout("dbg_wina1", [128, 6512], BF16)
                p.dbg_win_a1 = (dwa1, win2)
            if ps == 0:
                p.dbg_attn_hook = {
                    1: p.dout("dbg_ah1", [128, 128], BF16),
                    2: p.dout("dbg_ah2", [128, 128], BF16),
                    4: p.dout("dbg_ah4", [128, 128], BF16),
                    8: p.dout("dbg_ah8", [128, 128], BF16)}
            else:
                p.dbg_attn_hook = None
        attn_m = emit_l0_attention(p, qhalf, ks, vs, ac, masksL0, ebuf,
                                   negBs, lnb_t.ap(), sw_t.ap(), b12s, osbuf,
                                   smalls, ready, slot_defs=PASS_SLOTS[ps],
                                   subln_defer=True)
        if debug:
            dh = p.dout("dbg_h128_g%d" % ps, [128, 128], BF16)
            b.op("sync", lambda e, dh=dh: e.dma_start(
                out=dh, in_=abh[:, :]),
                waits=(("dve", attn_m),), inc="dbg", n=16)
            if ps == 0:
                win = nc.alloc_sbuf_tensor_at("dbgwin", [128, 6512], BF16,
                                              offset=16512)
                dw_ = p.dout("dbg_win", [128, 6512], BF16)
                b.op("sync", lambda e, dw_=dw_, win=win: e.dma_start(
                    out=dw_, in_=win.ap()[:, :]),
                    waits=(("dve", attn_m),), inc="dbg", n=16)

    if debug:
        dbg_ac = p.dout("dbg_ac", [D, S], BF16)
        dbg_nb = p.dout("dbg_nb", [128, 32], F32)
        b.op("sync", lambda e: e.dma_start(
            out=dbg_ac.rearrange("(c p) t -> p c t", p=128),
            in_=sb3_like(ac, NTILE)), waits=(("dve", attn_m),), inc="dbg", n=16)
        b.op("sync", lambda e: e.dma_start(out=dbg_nb, in_=negBs.ap()[:, :]),
             waits=(("dve", attn_m),), inc="dbg", n=16)
    if debug:
        dh = p.dout("dbg_h128_d", [128, 128], BF16)
        b.op("sync", lambda e, dh=dh: e.dma_start(
            out=dh, in_=abh[:, :]),
            waits=(("dve", attn_m),), inc="dbg", n=16)
    subm = emit_l0_subln(p, ac, ebuf, sw_t.ap(), smalls,
                         [("dve", attn_m)])
    op_m, spill_cnt = emit_outproj2(p, woT, sb3_like(ac, NTILE), x0, x1f,
                                    _T(wbuf), op_xold, ("dve", subm),
                                    [0, 1024])

    o_cnt = b.v("osem")
    for H in (0, 1):
        lsem = "l2x%d" % H
        w = [("spill", spill_cnt), ("dve", op_m)]
        if H == 1:
            w.append(("osem", o_cnt))
        b.op("sync", lambda e, H=H: e.dma_start(
            out=xt2.rearrange("p (c t) -> p c t", c=NTILE),
            in_=x1f[:, H * 1024:(H + 1) * 1024].rearrange(
                "(c p) t -> p c t", p=128)), waits=w, inc=lsem, n=16)
        lv = b.v(lsem)
        ln2_m = emit_layernorm(p, xt2.rearrange("p (c t) -> p c t", c=NTILE),
                               xn2.rearrange("p (c t) -> p c t", c=NTILE),
                               stats4, rbuf2, entry=((lsem, lv),))
        if debug:
            dbg_xn2 = p.dout("dbg_xn2_%d" % H, [D, T], BF16)
            b.op("sync", lambda e, dbg_xn2=dbg_xn2: e.dma_start(
                out=dbg_xn2.rearrange("(c p) t -> p c t", p=128),
                in_=xn2.rearrange("p (c t) -> p c t", c=NTILE)),
                waits=(("dve", ln2_m),), inc="dbg", n=16)
            dv = b.v("dbg")
            b.op("dve", lambda e: None, waits=(("dbg", dv),))
            b.op("dve", lambda e: e.engine_nop(), inc="dve")
            ln2_m = b.v("dve")
        dbgF = None
        if debug and H == 0:
            dbgF = {"hid_pre": p.dout("hid_pre", [8192, T], BF16),
                    "hid_post": p.dout("hid_post", [8192, T], BF16)}
            dbg_h = p.dout("dbg_h128", [128, 128], BF16)
            b.op("sync", lambda e, dbg_h=dbg_h: e.dma_start(
                out=dbg_h, in_=abh[:, :]),
                waits=(("dve", ln2_m),), inc="dbg", n=16)
            p.dbg_fwht = (p.dout("dbg_sc0", [128, 8192], F32),
                          p.dout("dbg_sc1", [128, 8192], F32))
        o_cnt = emit_ffn0(p, w1T, w3T, w2T, xn2, hidden, _T(wbuf), w2buf,
                          x1f[:, H * 1024:(H + 1) * 1024],
                          x2f[:, H * 1024:(H + 1) * 1024],
                          f_xold, x2ring, fwht_sc,
                          ("dve", ln2_m), b.v("spill"), silu_buf=silu_b,
                          dbg=dbgF)

    if only_ab:
        b.op("sync", lambda e: None, waits=(("osem", o_cnt),))
        return p

    p.new_phase("C_")
    ob_sb = p.sb("obsb", [1, 4], I32)
    qg_sb = p.sb("qgsb", [1, 16], F32)
    p.lrbuf = p.sb("lrbuf", [128, 256], BF16)
    vB_sb = p.sb("vBsb", [16, 512], BF16)
    kB_sb = p.sb("kBsb", [16, 512], BF16)
    qB_sb = p.sb("qBsb", [16, D], BF16)
    l1small = p.sb("l1small", [1, 1024], F32)
    p.l1small = l1small.ap()
    c1o = p.sb("c1o", [64, T], BF16)
    s1o = p.sb("s1o", [64, T], BF16)
    c1g = p.sb("c1g", [64, S], BF16)
    s1g = p.sb("s1g", [64, S], BF16)
    uq_t = p.sb("uqt", [16, T], BF16)
    uk2_t = p.sb("uk2t", [16, S], BF16)
    uv2_t = p.sb("uv2t", [16, S], BF16)
    uvo_t = p.sb("uvot", [16, T], BF16)
    l1pre = p.sb("l1pre", [128, 2 * T], BF16)
    p.l1pre = l1pre.ap()
    l1sq = p.sb("l1sq", [128, 2 * T], BF16)
    p.l1sq = l1sq.ap()
    carena = p.sb("carena", [128, 77824], BF16)
    car = carena.ap()
    carf = car.bitcast(F32)
    cxn = car[:, 0:32768]
    cxt = carf[:, 16384:32768]
    cstats = carf[0:1, 32768:35840]
    crbuf = car[:, 71680:73728]
    xn_own = car[:, 32768:49152]
    qring1 = car[:, 49152:51200]
    kring1 = car[:, 51200:55296]
    vtring1 = car[:, 55296:56320]
    vhring1 = car[:, 56320:58368]
    vf_own = car[:, 58368:62464]
    cvwbuf = car[:, 65536:73728]
    cwbuf = car[:, 73728:77824]
    cxnf3 = cxn.rearrange("p (c t) -> p c t", c=NTILE)
    cxt3 = cxt.rearrange("p (c t) -> p c t", c=NTILE)
    cspans = [(cxnf3[:, :, 0:1024], 0), (cxnf3[:, :, 1024:2048], 1024)]
    xn_own3 = xn_own.rearrange("p (c t) -> p c t", c=NTILE)

    b.op("sync", lambda e: e.dma_start(out=ob_sb.ap()[:, :], in_=ob_in),
         inc="obd", n=16)
    b.op("sync", lambda e: e.dma_start(out=qg_sb.ap()[:, :], in_=qg_in),
         inc="cdma", n=16)
    b.op("pool", lambda e: e.dma_start(out=vB_sb.ap()[:, :], in_=vB),
         inc="cdma", n=16)
    b.op("pool", lambda e: e.dma_start(out=kB_sb.ap()[:, :], in_=kB),
         inc="cdma", n=16)
    b.op("pool", lambda e: e.dma_start(out=qB_sb.ap()[:, :], in_=qB),
         inc="cdma", n=16)
    b.op("sync", lambda e: e.dma_start(out=c1o.ap()[:, :], in_=r1co_in),
         inc="cdma", n=16)
    b.op("sync", lambda e: e.dma_start(out=s1o.ap()[:, :], in_=r1so_in),
         inc="cdma", n=16)
    b.op("sync", lambda e: e.dma_start(out=c1g.ap()[:, :], in_=r1cg_in),
         inc="cdma", n=16)
    b.op("sync", lambda e: e.dma_start(out=s1g.ap()[:, :], in_=r1sg_in),
         inc="cdma", n=16)
    obd = b.v("obd")
    dynblk = {}
    b.op("pool", lambda e: dynblk.__setitem__(
        "p0", e.value_load(ob_sb.ap()[0:1, 0:1])), waits=(("obd", obd),))
    b.op("pool", lambda e: dynblk.__setitem__(
        "p1", e.value_load(ob_sb.ap()[0:1, 1:2])))

    b.op("pool", lambda e: e.dma_start(
        out=x2own[:, 0:512], in_=x2f[:, bass.ts(dynblk["p0"], 512)]),
        inc="x2od", n=16)
    b.op("pool", lambda e: e.dma_start(
        out=x2own[:, 512:1024], in_=x2f[:, bass.ts(dynblk["p1"], 512)]),
        inc="x2od", n=16)

    ln3_m = {}
    cdv = b.v("cdma")
    for H in (0, 1):
        lsem = "l3x%d" % H
        w = [("dve", ln3_m[0])] if H == 1 else []
        b.op("sync", lambda e, H=H: e.dma_start(
            out=cxt3, in_=x2f[:, H * 1024:(H + 1) * 1024].rearrange(
                "(c p) t -> p c t", p=128)), waits=w, inc=lsem, n=16)
        lv = b.v(lsem)
        entry = ((lsem, lv), ("cdma", cdv)) if H == 0 else ((lsem, lv),)
        ln3_m[H] = emit_layernorm(p, cxt3, cxnf3[:, :, H * 1024:(H + 1) * 1024],
                                  cstats, crbuf, entry=entry)
        b.op("sync", lambda e, H=H: e.dma_start(
            out=xn_d[:, H * 1024:(H + 1) * 1024].rearrange(
                "(c p) t -> p c t", p=128),
            in_=cxnf3[:, :, H * 1024:(H + 1) * 1024]),
            waits=(("dve", ln3_m[H]),), inc="xnd", n=16)
    xndv = b.v("xnd")
    b.op("pool", lambda e: e.dma_start(
        out=xnown_d[:, 0:512], in_=xn_d[:, bass.ts(dynblk["p0"], 512)]),
        waits=(("xnd", xndv),), inc="xodd", n=16)
    b.op("pool", lambda e: e.dma_start(
        out=xnown_d[:, 512:1024], in_=xn_d[:, bass.ts(dynblk["p1"], 512)]),
        inc="xodd", n=16)
    xoddv = b.v("xodd")
    b.op("sync", lambda e: e.dma_start(
        out=xn_own3, in_=xnown_d.rearrange("(c p) t -> p c t", p=128)),
        waits=(("xodd", xoddv),), inc="xod", n=16)
    xodv = b.v("xod")

    emit_l1_kvq2(
        p, cspans, xn_own3, wq1, wk1, wv1, qA, qB_sb.ap(), kA, kB_sb.ap(),
        vA, vB_sb.ap(), qg_sb.ap(),
        sb3_like(qring1, 2), q1_h, sb3_like(kring1, 2), k1f,
        sb3_like(vtring1, 2), v1f, sb3_like(vf_own, 4),
        sb3_like(vhring1, 2), vhat_h,
        uq_t.ap(), uk2_t.ap(), uv2_t.ap(), uvo_t.ap(), uv_d, dynblk,
        _T(cwbuf), cvwbuf, ("xod", xodv), ("dve", ln3_m[1]),
        (c1o.ap(), s1o.ap()), (c1g.ap(), s1g.ap()))

    if only_abc:
        waits = tuple((s, b.v(s)) for s in b.order if b.v(s) > 0)
        b.op("sync", lambda e: None, waits=waits)
        return p

    p.new_phase("D_")
    ioD = {"x": x2own, "q1": q1_h, "k1full": k1f, "v1full": v1f,
           "vhat": vhat_h, "masks": m1_in, "qgain": qg_in, "wproj": wproj,
           "pA": pA, "pB": pB, "wupT": wupT, "wgateT": wgateT,
           "wdownT": wdownT, "su_ug": su_ug, "su_dp": su_dp, "ws_up": ws_up,
           "ws_gate": ws_gate, "ws_dp": ws_dp, "h128_in": h128_in,
           "out": out_x, "x3": x3_h}
    build_l1_rest(debug=False, p=p, io=ioD)
    return p


def _rope0_global():
    pos = np.arange(S).astype(np.float64)
    invf = 10000.0 ** (-np.arange(64) / 64.0)
    fr = pos[None, :] * invf[:, None]
    c = np.cos(fr); s = np.sin(fr)
    C = np.concatenate([c, c], 0).astype(BF)
    blk = np.concatenate([s[32:64], -s[0:32]], 0)
    S2 = np.concatenate([blk, blk], 0).astype(BF)
    return C, S2


def _rope1_global():
    pos = np.arange(S).astype(np.float64)
    invf = 1.0 / 10000.0 ** (np.arange(0, 64, 2) / 64.0)
    fr = pos[None, :] * invf[:, None]
    c = np.cos(fr); s = np.sin(fr)
    C1 = np.concatenate([c, c], 0).astype(BF)
    SS = np.concatenate([-s, s], 0).astype(BF)
    return C1, SS


def _masks0():
    out = np.zeros((4, 128, 512), dtype=BF)
    for r in range(4):
        i = 128 * r + np.arange(128)[:, None]
        j = np.arange(512)[None, :]
        out[r] = (i <= j).astype(np.float32).astype(BF)
    return np.ascontiguousarray(out.reshape(512, 512))


OWN_BLKS = {0: (3, 0), 1: (2, 1)}


def kernel_fused2(**inputs):
    inp = {k: np.asarray(v) for k, v in inputs.items()}
    f32 = np.float32
    cont = np.ascontiguousarray
    x_full = inp["L0_attn_hidden_states"].astype(f32)
    r0g = _rope0_global()
    r1g = _rope1_global()
    shared = {
        "wqT": cont(inp["L0_attn_Wq"].T.astype(f32)),
        "wkT": cont(inp["L0_attn_Wk"].T.astype(f32)),
        "wvT": cont(inp["L0_attn_Wv"].T.astype(f32)),
        "woT": cont(inp["L0_attn_Wo"].T.astype(f32)),
        "w1T": cont(inp["L0_ffn_W1"].T.astype(f32)),
        "w3T": cont(inp["L0_ffn_W3"].T.astype(f32)),
        "w2T": cont(inp["L0_ffn_W2"].T.astype(f32)),
        "lam": np.concatenate([
            inp["L0_attn_lambda_q1"], inp["L0_attn_lambda_k1"],
            inp["L0_attn_lambda_q2"], inp["L0_attn_lambda_k2"]])[None, :].astype(f32),
        "sublnw": inp["L0_attn_subln_weight"].reshape(128, 1).astype(f32),
        "rope0cG": r0g[0], "rope0sG": r0g[1],
        "masks0": _masks0(),
        "h128_in": _hadamard128(),
        "wq": cont(inp["L1_attn_Wq"].astype(f32)),
        "wk": cont(inp["L1_attn_Wk"].astype(f32)),
        "wv": cont(inp["L1_attn_Wv"].astype(f32)),
        "qA": cont(inp["L1_attn_q_A"].astype(f32)),
        "qB": cont(inp["L1_attn_q_B"].astype(f32)),
        "kA": cont(inp["L1_attn_k_A"].astype(f32)),
        "kB": cont(inp["L1_attn_k_B"].astype(f32)),
        "vA": cont(inp["L1_attn_v_A"].astype(f32)),
        "vB": cont(inp["L1_attn_v_B"].astype(f32)),
        "qgain": inp["L1_attn_q_gain"][None, :].astype(f32),
        "rope1cG": r1g[0], "rope1sG": r1g[1],
        "wproj": cont(inp["L1_attn_Wproj"].astype(f32)),
        "pA": cont(inp["L1_attn_p_A"].astype(f32)),
        "pB": cont(inp["L1_attn_p_B"].astype(f32)),
        "wupT": cont(inp["L1_ffn_W_up"].T.astype(f32)),
        "wgateT": cont(inp["L1_ffn_W_gate"].T.astype(f32)),
        "wdownT": cont(inp["L1_ffn_W_down"].T.astype(f32)),
        "su_ug": cont(inp["L1_ffn_SU_ug"].reshape(16, 128).T.astype(f32)),
        "su_dp": cont(inp["L1_ffn_SU_dp"].reshape(64, 128).T.astype(f32)),
        "ws_up": cont(inp["L1_ffn_Wscale_ug"][:8192].reshape(64, 128).T.astype(f32)),
        "ws_gate": cont(inp["L1_ffn_Wscale_ug"][8192:].reshape(64, 128).T.astype(f32)),
        "ws_dp": cont(inp["L1_ffn_Wscale_dp"].reshape(16, 128).T.astype(f32)),
    }
    pos = {h: own_pos(h) for h in (0, 1)}
    r1o = {h: rope1_tables(h) for h in (0, 1)}
    mk1 = {h: np.ascontiguousarray(mask_tables(h).reshape(2048, 512))
           for h in (0, 1)}
    nc = _get_nc("F2", lambda debug=False: build_fused2())
    maps = []
    for c in range(8):
        bb, h = divmod(c, 2)
        m = dict(shared)
        m["x"] = cont(x_full[bb].T)
        m["rope1cO"] = r1o[h][0]; m["rope1sO"] = r1o[h][1]
        m["masks"] = mk1[h]
        m["ownblk"] = np.array([[OWN_BLKS[h][0], OWN_BLKS[h][1], 0, 0]],
                               np.int32)
        maps.append(m)
    res = run_bass_kernel_spmd(nc, maps, core_ids=list(range(8)))
    out = np.empty((4, 2048, 2048), dtype=np.float32)
    for c in range(8):
        bb, h = divmod(c, 2)
        out[bb][pos[h]] = res.results[c]["out"].T
    return out
